# version 09
# baseline (speedup 1.0000x reference)
import numpy as np
import concourse.bass as bass
import concourse.mybir as mybir
from concourse.bass_utils import run_bass_kernel_spmd
from contextlib import ExitStack

F32 = mybir.dt.float32
BF16 = mybir.dt.bfloat16
AF = mybir.ActivationFunctionType
ALU = mybir.AluOpType
AX = mybir.AxisListType

B, S, D = 4, 4096, 2048
HD, NH, NKV = 128, 16, 4
EXPERTS = 16
THETA = 10000.0
SCALE_BASE = 512.0
LN_EPS = 1e-5
EXP_SHIFT = 30.0
MAGIC = 12582912.0
TOWN = 2048
HALF = 1024
INV2PI = 1.0 / (2.0 * np.pi)

ENGINES = ("tensor", "vector", "scalar", "gpsimd", "sync")
C_UNITS = None


class Prog:

    def __init__(self, sched, on=True):
        self.q = {e: [] for e in ENGINES}
        self.sched = sched
        self.on = on

    def val(self, sem):
        return self.sched.get(sem, 0)

    def emit(self, eng, fn, wait=(), inc=()):
        if not self.on:
            return {sname: 0 for sname, _ in inc}
        out = {}
        incs = []
        for sname, n in inc:
            v = self.sched.get(sname, 0) + n
            self.sched[sname] = v
            out[sname] = v
            incs.append((sname, n))
        wait = tuple((sn, vv) for sn, vv in wait if vv and vv > 0)
        self.q[eng].append((fn, wait, tuple(incs)))
        return out

    def fence(self, engines=ENGINES):
        snap = [(sn, v) for sn, v in self.sched.items() if v > 0]
        for e in engines:
            self.emit(e, lambda x: x.nop(), wait=snap)

    def flush(self, block, sems, waited):
        def runner(ename):
            def run(engobj):
                w = waited[ename]
                for fn, waits, incs in self.q[ename]:
                    for sname, v in waits:
                        if v > w.get(sname, 0):
                            engobj.wait_ge(sems[sname], v)
                            w[sname] = v
                    inst = fn(engobj)
                    for sname, n in incs:
                        inst = inst.then_inc(sems[sname], n)
            return run

        for e in ENGINES:
            if self.q[e]:
                getattr(block, e)(runner(e))
        self.q = {e: [] for e in ENGINES}


def _ts(i, w=128):
    return bass.ts(i, w)


def build_program(dbg=(), phases=("A", "B", "B2", "C", "D")):
    nc = bass.Bass("TRN2", target_bir_lowering=False)
    dbg = set(dbg)
    phases = set(phases)

    x_own = nc.dram_tensor("x_own", [TOWN, D], F32, kind="ExternalInput").ap()
    x_oth = nc.dram_tensor("x_oth", [TOWN, D], F32, kind="ExternalInput").ap()
    posmeta = nc.dram_tensor("posmeta", [128, 16], F32, kind="ExternalInput").ap()
    gw_r = nc.dram_tensor("gw_r", [16, 128, 16], F32, kind="ExternalInput").ap()
    wq_r = nc.dram_tensor("wq_r", [NH, 16, 128, 128], F32, kind="ExternalInput").ap()
    wk_r = nc.dram_tensor("wk_r", [NKV, 16, 128, 128], F32, kind="ExternalInput").ap()
    wv_r = nc.dram_tensor("wv_r", [NKV, 16, 128, 128], F32, kind="ExternalInput").ap()
    wo_r = nc.dram_tensor("wo_r", [16, 16, 128, 128], F32, kind="ExternalInput").ap()
    wgu_r = nc.dram_tensor("wgu_r", [EXPERTS, 22, 16, 128, 128], F32, kind="ExternalInput").ap()
    wd_r = nc.dram_tensor("wd_r", [EXPERTS, 11, 128, D], F32, kind="ExternalInput").ap()
    sgu_r = nc.dram_tensor("sgu_r", [44, 16, 128, 128], F32, kind="ExternalInput").ap()
    sdn_r = nc.dram_tensor("sdn_r", [22, 128, D], F32, kind="ExternalInput").ap()
    out_t = nc.dram_tensor("out", [TOWN, D], F32, kind="ExternalOutput").ap()

    def dscratch(name, shape, dtype, key):
        kind = "ExternalOutput" if key in dbg else "Internal"
        return nc.dram_tensor(name, shape, dtype, kind=kind).ap()

    xn_hbm = dscratch("xn_hbm", [16, 128, TOWN], BF16, "xn")
    s_hbm = dscratch("s_hbm", [16, 128, TOWN], BF16, "s")
    comb_hbm = dscratch("comb_hbm", [16, TOWN], F32, "comb")
    dbg_outs = {}
    for key, shape, dt in [("tab", [128, 2048], F32),
                           ("krope", [128, NKV * S], BF16), ("vtm", [128, NKV * S], BF16),
                           ("o", [128, 16 * TOWN], BF16), ("y0", [128, 16 * HALF], F32),
                           ("y1", [128, 16 * HALF], F32)]:
        if key in dbg:
            dbg_outs[key] = nc.dram_tensor("dbg_" + key, shape, dt, kind="ExternalOutput").ap()

    sched = {}
    sems = {}
    waited = {e: {} for e in ENGINES}

    es = ExitStack()
    with es:
        for sname in ("ws", "wg", "pe", "dv", "ac", "go"):
            sems[sname] = es.enter_context(nc.semaphore(sname))
        block = es.enter_context(nc.Block())

        def sbg(name, shape, dtype):
            return es.enter_context(nc.sbuf_tensor(name, shape, dtype))

        ident_bf = sbg("ident_bf", [128, 128], BF16)
        ident_f = sbg("ident_f", [128, 128], F32)
        swap_bf = sbg("swap_bf", [128, 128], BF16)
        ones_col_bf = sbg("ones_col_bf", [128, 1], BF16)
        ones_row_f = sbg("ones_row_f", [1, 128], F32)
        iota_p = sbg("iota_p", [128, 1], F32)
        cols_c = sbg("cols_c", [128, 12], F32)
        pm_sb = sbg("pm_sb", [128, 16], F32)
        gw_sb = sbg("gw_sb", [128, 256], F32)
        pb = [es.enter_context(nc.psum_tensor(f"pb{i}", [128, 512], F32)) for i in range(7)]
        pbt = es.enter_context(nc.psum_tensor("pbt", [128, 1024], BF16))

        PARC = cols_c[:, 0:1]
        NPAR = cols_c[:, 1:2]
        SIGN = cols_c[:, 2:3]
        INVF = cols_c[:, 3:4]
        TMPC = cols_c[:, 4:5]
        TMPC2 = cols_c[:, 5:6]
        B_L2PI = cols_c[:, 6:7]
        B_QX = cols_c[:, 7:8]
        B_EPS = cols_c[:, 8:9]
        B_NSH = cols_c[:, 9:10]

        pr = Prog(sched)
        pr.emit("sync", lambda e: e.dma_start(out=pm_sb[:], in_=posmeta), inc=[("ws", 16)])
        pr.emit("sync", lambda e: e.dma_start(out=gw_sb[:].rearrange("p (c e) -> p c e", c=16), in_=gw_r.rearrange("c p e -> p c e")),
                inc=[("ws", 16)])
        ws_consts = pr.val("ws")

        def _init_g(g):
            n = [0]

            def step(fn):
                fn().then_inc(sems["go"], 1)
                n[0] += 1
                g.wait_ge(sems["go"], n[0])
            step(lambda: g.memset(ident_bf[:], 1.0))
            step(lambda: g.affine_select(ident_bf[:], ident_bf[:], pattern=[[-1, 128]],
                                         compare_op=ALU.is_equal, fill=0.0, base=0,
                                         channel_multiplier=1))
            step(lambda: g.memset(ident_f[:], 1.0))
            step(lambda: g.affine_select(ident_f[:], ident_f[:], pattern=[[-1, 128]],
                                         compare_op=ALU.is_equal, fill=0.0, base=0,
                                         channel_multiplier=1))
            step(lambda: g.memset(swap_bf[:], 1.0))
            step(lambda: g.affine_select(swap_bf[:, 0:64], swap_bf[:, 0:64], pattern=[[-1, 64]],
                                         compare_op=ALU.is_equal, fill=0.0, base=-64,
                                         channel_multiplier=1))
            step(lambda: g.affine_select(swap_bf[:, 64:128], swap_bf[:, 64:128],
                                         pattern=[[-1, 64]], compare_op=ALU.is_equal,
                                         fill=0.0, base=0, channel_multiplier=1))
            step(lambda: g.memset(ones_col_bf[:], 1.0))
            step(lambda: g.memset(ones_row_f[:], 1.0))
            step(lambda: g.memset(B_L2PI, float(np.log(INV2PI))))
            step(lambda: g.memset(B_QX, float(-0.5 * np.log(HD))))
            step(lambda: g.memset(B_EPS, LN_EPS))
            step(lambda: g.memset(B_NSH, -EXP_SHIFT))
            step(lambda: g.iota(iota_p[:], pattern=[[0, 1]], base=0, channel_multiplier=1,
                                allow_small_or_imprecise_dtypes=True))
            return g.drain()
        sched["go"] = sched.get("go", 0) + 14
        go_init = pr.emit("gpsimd", _init_g, inc=[("go", 1)])["go"]

        def _chain(pr, eng, fns, wait0=()):
            last = 0
            for i, fn in enumerate(fns):
                last = pr.emit(eng, fn, wait=(list(wait0) if i == 0 else [])
                               + [("dv" if eng == "vector" else "ac", last)],
                               inc=[("dv" if eng == "vector" else "ac", 1)])[
                                   "dv" if eng == "vector" else "ac"]
            return last

        dv_init = _chain(pr, "vector", [
            lambda v: v.tensor_scalar(TMPC, iota_p[:], 0.5, -0.25, op0=ALU.mult, op1=ALU.add),
            lambda v: v.tensor_scalar_add(TMPC, TMPC, MAGIC),
            lambda v: v.tensor_scalar_sub(TMPC, TMPC, MAGIC),
            lambda v: v.scalar_tensor_tensor(PARC, TMPC, 2.0, iota_p[:],
                                             op0=ALU.mult, op1=ALU.subtract),
            lambda v: v.tensor_scalar(PARC, PARC, 2.0, 1.0, op0=ALU.mult, op1=ALU.add),
            lambda v: v.tensor_scalar_mul(NPAR, PARC, -1.0),
            lambda v: v.tensor_scalar(SIGN, iota_p[:], 64.0, None, op0=ALU.is_ge),
            lambda v: v.tensor_scalar(SIGN, SIGN, 2.0, -1.0, op0=ALU.mult, op1=ALU.add),
            lambda v: v.tensor_scalar(TMPC, iota_p[:], 64.0, None, op0=ALU.is_ge),
            lambda v: v.scalar_tensor_tensor(TMPC, TMPC, 64.0, iota_p[:],
                                             op0=ALU.mult, op1=ALU.subtract),
            lambda v: v.tensor_scalar_mul(TMPC, TMPC, np.log(THETA) / 64.0),
        ], wait0=[("go", go_init)])
        pr.emit("scalar", lambda s: s.activation(INVF, TMPC, AF.Exp, bias=B_L2PI),
                wait=[("dv", dv_init)], inc=[("ac", 1)])
        pr.fence()
        pr.flush(block, sems, waited)

        PM_GO = [pm_sb[:, j:j + 1] for j in range(4)]
        PM_GT = [pm_sb[:, 4 + j:5 + j] for j in range(4)]

        def rope_tables(pr, base_col, w, cosd, sind, gen, for_q):
            A, Bv, C, Dv = (g[:] for g in gen)
            cosd = cosd[:]
            sind = sind[:]
            pr.emit("gpsimd", lambda g: g.iota(A, pattern=[[1, w]], base=0, channel_multiplier=0,
                                               allow_small_or_imprecise_dtypes=True),
                    wait=[("go", go_init), ("dv", pr.val("dv")), ("ac", pr.val("ac"))],
                    inc=[("go", 1)])
            gv = pr.val("go")
            pr.emit("vector", lambda v: v.tensor_scalar_add(A, A, base_col),
                    wait=[("go", gv), ("ws", ws_consts)])
            pr.emit("vector", lambda v: v.tensor_scalar_mul(Bv, A, INVF))
            pr.emit("vector", lambda v: v.tensor_scalar_add(C, Bv, MAGIC))
            pr.emit("vector", lambda v: v.tensor_scalar_sub(C, C, MAGIC))
            pr.emit("vector", lambda v: v.tensor_sub(C, Bv, C))
            dvv = pr.emit("vector", lambda v: v.tensor_scalar_mul(C, C, 2.0 * np.pi),
                          inc=[("dv", 1)])["dv"]
            ac_s = pr.emit("scalar", lambda s: s.activation(sind, C, AF.Sin),
                           wait=[("dv", dvv)], inc=[("ac", 1)])["ac"]
            pr.emit("vector", lambda v: v.tensor_scalar_add(Dv, Bv, 0.25),
                    wait=[("ac", ac_s)])
            pr.emit("vector", lambda v: v.tensor_scalar_add(C, Dv, MAGIC))
            pr.emit("vector", lambda v: v.tensor_scalar_sub(C, C, MAGIC))
            pr.emit("vector", lambda v: v.tensor_sub(C, Dv, C))
            dvv = pr.emit("vector", lambda v: v.tensor_scalar_mul(C, C, 2.0 * np.pi),
                          inc=[("dv", 1)])["dv"]
            ac_c = pr.emit("scalar", lambda s: s.activation(cosd, C, AF.Sin),
                           wait=[("dv", dvv)], inc=[("ac", 1)])["ac"]
            dvv = pr.emit("vector", lambda v: v.tensor_scalar(
                Bv, A, 1.0 / SCALE_BASE, 0.5, op0=ALU.mult, op1=ALU.add),
                wait=[("ac", ac_c)], inc=[("dv", 1)])["dv"]
            ac_l = pr.emit("scalar", lambda s: s.activation(Bv, Bv, AF.Ln),
                           wait=[("dv", dvv)], inc=[("ac", 1)])["ac"]
            par = PARC if for_q else NPAR
            dvv = pr.emit("vector", lambda v: v.tensor_scalar_mul(A, Bv, par),
                          wait=[("ac", ac_l)], inc=[("dv", 1)])["dv"]
            if for_q:
                ac_x = pr.emit("scalar", lambda s: s.activation(A, A, AF.Exp, bias=B_QX),
                               wait=[("dv", dvv)], inc=[("ac", 1)])["ac"]
            else:
                ac_x = pr.emit("scalar", lambda s: s.activation(A, A, AF.Exp),
                               wait=[("dv", dvv)], inc=[("ac", 1)])["ac"]
            pr.emit("vector", lambda v: v.tensor_mul(cosd, cosd, A), wait=[("ac", ac_x)])
            pr.emit("vector", lambda v: v.tensor_scalar_mul(A, A, SIGN))
            return pr.emit("vector", lambda v: v.tensor_mul(sind, sind, A),
                           inc=[("dv", 1)])["dv"]

        esAB = ExitStack()
        with esAB:
            krope = esAB.enter_context(nc.sbuf_tensor("krope", [128, NKV * S], BF16))
            vtm = esAB.enter_context(nc.sbuf_tensor("vtm", [128, NKV * S], BF16))

            esA = ExitStack()
            with esA if "A" in phases else ExitStack():
              if "A" in phases:
                def sba(name, shape, dtype):
                    return esA.enter_context(nc.sbuf_tensor(name, shape, dtype))

                wk_bf = sba("wk_bf", [128, NKV * 16 * 128], BF16)
                wv_bf = sba("wv_bf", [128, NKV * 16 * 128], BF16)
                xt = [sba(f"xt{i}", [128, D], F32) for i in range(2)]
                lnst = sba("lnst", [128, 48], F32)
                lncol = sba("lncol", [128, 16], F32)
                xn_tm = [sba(f"xn_tm{i}", [128, D], F32) for i in range(2)]
                xn_grp = [sba(f"xn_grp{i}", [128, 16 * 512], BF16) for i in range(2)]
                xnf32 = sba("xnf32", [128, 16 * 128], F32)
                tabs = [(sba(f"ckt{i}", [128, 512], F32), sba(f"skt{i}", [128, 512], F32))
                        for i in range(2)]
                genS = [sba(f"genS{i}", [128, 512], F32) for i in range(4)]
                kv_sb = [sba(f"kv_sb{i}", [128, 512], BF16) for i in range(2)]
                kv_sw = sba("kv_sw", [128, 512], BF16)
                kv_t1 = sba("kv_t1", [128, 512], F32)
                kv_t2 = sba("kv_t2", [128, 512], F32)
                lg_sb = sba("lg_sb", [16, 128], F32)
                lt_sb = sba("lt_sb", [128, 16], F32)
                gcol = sba("gcol", [128, 8], F32)
                gm_sb = sba("gm_sb", [128, 16], F32)

                pr = Prog(sched, on="A" in phases)
                pr.emit("gpsimd", lambda g: g.dma_start(
                    out=wk_bf[:].rearrange("p (k c j) -> p k c j", k=NKV, c=16), in_=wk_r.rearrange("k c p j -> p k c j")),
                    wait=[("go", go_init)], inc=[("wg", 16)])
                pr.emit("gpsimd", lambda g: g.dma_start(
                    out=wv_bf[:].rearrange("p (k c j) -> p k c j", k=NKV, c=16), in_=wv_r.rearrange("k c p j -> p k c j")),
                    inc=[("wg", 16)])
                wg_kv = pr.val("wg")

                xt_free = [(0, 0), (0, 0)]
                xtm_free = [0, 0]
                xng_free = [0, 0]
                xng_ws_free = [0, 0]
                tp_free = [(0, 0), (0, 0)]
                ci_free = [(0, 0)] * 4
                glog_free = 0
                gtp_free = 0
                xnf32_free = 0
                pb4_free_ac = 0
                kvsb0_free_pe = 0
                kvsb0_free_dv = 0
                kvsw_free_dv = 0
                kvsb1_free_pe = 0
                pbt_free = 0
                tab_ready = [0, 0]

                ln_state = {}

                def emit_ln(t, src_t, row16):
                    bi = t % 2
                    ci = t % 4
                    CS = lncol[:, 4 * ci:4 * ci + 4]
                    c_mean, c_var, c_rstd = (CS[:, 0:1], CS[:, 1:2], CS[:, 3:4])
                    row0 = row16 * 128
                    x_rdy = pr.emit("sync", lambda e, bi=bi, src_t=src_t, row0=row0: e.dma_start(
                        out=xt[bi][:], in_=src_t[row0:row0 + 128, :]),
                        wait=[("dv", xt_free[bi][0]), ("ac", xt_free[bi][1])],
                        inc=[("ws", 16)])["ws"]
                    bnb = lnst[:, 24 * (t % 2):24 * (t % 2) + 24]
                    fns = [
                        (lambda v, bi=bi, j=j, bnb=bnb: v.bn_stats(
                            bnb[:, j * 6:(j + 1) * 6], xt[bi][:, j * 512:(j + 1) * 512]))
                        for j in range(4)
                    ] + [lambda v, bnb=bnb, ci=ci: v.bn_aggr(
                        lncol[:, 4 * ci:4 * ci + 2], bnb)]
                    _chain(pr, "vector", fns,
                           wait0=[("ws", x_rdy), ("dv", ci_free[ci][0]),
                                  ("ac", ci_free[ci][1])])
                    dv_var = pr.val("dv")
                    pr.emit("scalar", lambda s, a=c_var: s.activation(a, a, AF.Ln, bias=B_EPS),
                            wait=[("dv", dv_var)])
                    ac_rstd = pr.emit("scalar", lambda s, a=c_var, b=c_rstd: s.activation(
                        b, a, AF.Exp, scale=-0.5), inc=[("ac", 1)])["ac"]
                    dv_norm = pr.emit("vector", lambda v, bi=bi, a=c_mean, b=c_rstd:
                                      v.tensor_scalar(xn_tm[bi][:], xt[bi][:], a, b,
                                                      op0=ALU.subtract, op1=ALU.mult),
                                      wait=[("ac", ac_rstd), ("pe", xtm_free[bi])],
                                      inc=[("dv", 1)])["dv"]
                    xt_free[bi] = (dv_norm, 0)
                    ci_free[ci] = (dv_norm, ac_rstd)
                    ln_state[t] = dv_norm

                def emit_rest(t, own):
                    nonlocal xnf32_free, glog_free, gtp_free
                    bi = t % 2
                    ti = (t // 4) % 2
                    k4 = t % 4
                    dv_norm = ln_state.pop(t)
                    last_tp = 0
                    for dc in range(16):
                        pk = dc % 2
                        pe_tp = pr.emit("tensor", lambda tt, bi=bi, dc=dc, pk=pk: tt.transpose(
                            pb[pk][:, 0:128], xn_tm[bi][:, _ts(dc)], ident_f[:]),
                            wait=[("dv", dv_norm), ("dv", tp_free[pk][0]),
                                  ("ac", tp_free[pk][1])],
                            inc=[("pe", 1)])["pe"]
                        last_tp = pe_tp
                        dv_ev = pr.emit("vector", lambda v, ti=ti, dc=dc, k4=k4, pk=pk:
                                        v.tensor_copy(
                            xn_grp[ti][:, dc * 512 + k4 * 128: dc * 512 + k4 * 128 + 128],
                            pb[pk][:, 0:128]),
                            wait=[("pe", pe_tp), ("pe", xng_free[ti]),
                                  ("ws", xng_ws_free[ti])],
                            inc=[("dv", 1)])["dv"]
                        ac_ev = tp_free[pk][1]
                        if own:
                            ac_ev = pr.emit("scalar", lambda s, dc=dc, pk=pk: s.activation(
                                xnf32[:, _ts(dc)], pb[pk][:, 0:128], AF.Identity),
                                wait=[("pe", pe_tp), ("pe", xnf32_free), ("dv", dv_ev)],
                                inc=[("ac", 1)])["ac"]
                        tp_free[pk] = (dv_ev, ac_ev)
                    xtm_free[bi] = last_tp
                    if own:
                        last_ac = max(tp_free[0][1], tp_free[1][1])
                        for dc in range(16):
                            pr.emit("tensor", lambda tt, dc=dc: tt.matmul(
                                pb[2][0:16, 0:128], gw_sb[:, 16 * dc:16 * dc + 16],
                                xnf32[:, _ts(dc)], start=(dc == 0), stop=(dc == 15),
                                skip_group_check=True),
                                wait=([("ac", last_ac), ("dv", glog_free)]
                                      if dc == 0 else ()),
                                inc=[("pe", 1)] if dc == 15 else ())
                        pe_g = pr.val("pe")
                        xnf32_free = pe_g
                        dv_g = pr.emit("vector", lambda v: v.tensor_copy(
                            lg_sb[:], pb[2][0:16, 0:128]),
                            wait=[("pe", pe_g)], inc=[("dv", 1)])["dv"]
                        glog_free = dv_g
                        pe_gt = pr.emit("tensor", lambda tt: tt.transpose(
                            pb[3][0:128, 0:16], lg_sb[:], ident_f[0:16, 0:16]),
                            wait=[("dv", dv_g), ("dv", gtp_free)], inc=[("pe", 1)])["pe"]
                        dv_lt = pr.emit("vector", lambda v: v.tensor_copy(
                            lt_sb[:], pb[3][0:128, 0:16]),
                            wait=[("pe", pe_gt)], inc=[("dv", 1)])["dv"]
                        g_m1, g_m2 = gcol[:, 0:1], gcol[:, 1:2]
                        g_nm1, g_rs = gcol[:, 2:3], gcol[:, 3:4]
                        dv_pre = _chain(pr, "vector", [
                            lambda v, a=g_m1: v.reduce_max(a, lt_sb[:], axis=AX.X),
                            lambda v, a=g_m1: v.tensor_scalar(
                                gm_sb[:], lt_sb[:], a, None, op0=ALU.is_equal),
                            lambda v: v.scalar_tensor_tensor(
                                gm_sb[:], gm_sb[:], -1e30, lt_sb[:],
                                op0=ALU.mult, op1=ALU.add),
                            lambda v, a=g_m2: v.reduce_max(a, gm_sb[:], axis=AX.X),
                            lambda v, a=g_m2: v.tensor_scalar(
                                gm_sb[:], lt_sb[:], a, None, op0=ALU.is_ge),
                            lambda v, a=g_m1, b=g_nm1: v.tensor_scalar_mul(b, a, -1.0),
                        ], wait0=[("dv", dv_lt)])
                        ac_ex = pr.emit("scalar", lambda s, b=g_nm1: s.activation(
                            lt_sb[:], lt_sb[:], AF.Exp, bias=b),
                            wait=[("dv", dv_pre)], inc=[("ac", 1)])["ac"]
                        dv_cmb = _chain(pr, "vector", [
                            lambda v, a=g_rs: v.reduce_sum(a, lt_sb[:], axis=AX.X),
                            lambda v, a=g_rs: v.reciprocal(a, a),
                            lambda v: v.tensor_mul(gm_sb[:], gm_sb[:], lt_sb[:]),
                            lambda v, a=g_rs: v.tensor_scalar_mul(gm_sb[:], gm_sb[:], a),
                        ], wait0=[("ac", ac_ex)])
                        pe_ct = pr.emit("tensor", lambda tt: tt.transpose(
                            pb[3][0:16, 0:128], gm_sb[:], ident_f[:]),
                            wait=[("dv", dv_cmb)], inc=[("pe", 1)])["pe"]
                        dv_ce = pr.emit("vector", lambda v: v.tensor_copy(
                            lg_sb[0:16, 0:128], pb[3][0:16, 0:128]),
                            wait=[("pe", pe_ct), ("dv", dv_lt)], inc=[("dv", 1)])["dv"]
                        gtp_free = dv_ce
                        pr.emit("sync", lambda e, t=t: e.dma_start(
                            out=comb_hbm[:, t * 128:(t + 1) * 128], in_=lg_sb[0:16, 0:128]),
                            wait=[("dv", dv_ce)], inc=[("ws", 16)])

                for grp in range(8):
                    own = grp < 4
                    gl = grp % 4
                    src = x_own if own else x_oth
                    base_col = PM_GO[gl] if own else PM_GT[gl]
                    ti = grp % 2
                    ck, sk = tabs[ti]
                    tab_ready[ti] = rope_tables(pr, base_col, 512, ck, sk, genS, for_q=False)
                    if grp == 0 and "tab" in dbg_outs:
                        pr.emit("sync", lambda e: e.dma_start(
                            out=dbg_outs["tab"][:, 0:512], in_=ck[:]),
                            wait=[("dv", tab_ready[0])], inc=[("ws", 16)])
                        pr.emit("sync", lambda e: e.dma_start(
                            out=dbg_outs["tab"][:, 512:1024], in_=sk[:]), inc=[("ws", 16)])
                        pr.emit("sync", lambda e: e.dma_start(
                            out=dbg_outs["tab"][:, 1024:1536], in_=genS[0][:]), inc=[("ws", 16)])
                        pr.emit("sync", lambda e: e.dma_start(
                            out=dbg_outs["tab"][:, 1536:1548], in_=cols_c[:]), inc=[("ws", 16)])
                        pr.emit("sync", lambda e: e.dma_start(
                            out=dbg_outs["tab"][:, 1548:1549], in_=iota_p[:],
                            allow_slow_non_contiguous=True), inc=[("ws", 16)])
                        pr.emit("vector", lambda v: v.nop(), wait=[("ws", pr.val("ws"))])
                        pr.emit("gpsimd", lambda g: g.nop(), wait=[("ws", pr.val("ws"))])
                    grp_xn_dv = 0
                    for k4 in range(4):
                        t = grp * 4 + k4
                        if t == 0:
                            emit_ln(0, x_own, 0)
                        tn = t + 1
                        if tn < 32:
                            nsrc = x_own if tn < 16 else x_oth
                            emit_ln(tn, nsrc, tn % 16)
                        emit_rest(t, grp < 4)
                        grp_xn_dv = pr.val("dv")
                    for kv in range(NKV):
                        for dc in range(16):
                            pr.emit("tensor", lambda tt, kv=kv, dc=dc, ti=ti: tt.matmul(
                                pb[4][:], wk_bf[:, (kv * 16 + dc) * 128:(kv * 16 + dc + 1) * 128],
                                xn_grp[ti][:, dc * 512:(dc + 1) * 512],
                                start=(dc == 0), stop=(dc == 15), skip_group_check=True),
                                wait=([("dv", grp_xn_dv), ("wg", wg_kv),
                                       ("ac", pb4_free_ac)] if dc == 0 else ()),
                                inc=[("pe", 1)] if dc == 15 else ())
                        pe_k = pr.val("pe")
                        ac_k = pr.emit("scalar", lambda s: s.activation(
                            kv_sb[0][:], pb[4][:], AF.Identity),
                            wait=[("pe", pe_k), ("pe", kvsb0_free_pe),
                                  ("dv", kvsb0_free_dv)], inc=[("ac", 1)])["ac"]
                        pe_sw = pr.emit("tensor", lambda tt: tt.matmul(
                            pb[5][:], swap_bf[:], kv_sb[0][:], start=True, stop=True,
                            skip_group_check=True),
                            wait=[("ac", ac_k)], inc=[("pe", 1)])["pe"]
                        kvsb0_free_pe = pe_sw
                        ac_sw = pr.emit("scalar", lambda s: s.activation(
                            kv_sw[:], pb[5][:], AF.Identity),
                            wait=[("pe", pe_sw), ("dv", kvsw_free_dv)], inc=[("ac", 1)])["ac"]
                        pr.emit("vector", lambda v, ck=ck: v.tensor_mul(kv_t1[:], kv_sb[0][:], ck[:]),
                                wait=[("ac", ac_sw), ("dv", tab_ready[ti])])
                        pr.emit("vector", lambda v, sk=sk: v.tensor_mul(kv_t2[:], kv_sw[:], sk[:]))
                        col0 = kv * S + (0 if own else TOWN) + gl * 512
                        dv_kr = pr.emit("vector", lambda v, col0=col0: v.tensor_add(
                            krope[:, col0:col0 + 512], kv_t1[:], kv_t2[:]),
                            inc=[("dv", 1)])["dv"]
                        kvsb0_free_dv = dv_kr
                        kvsw_free_dv = dv_kr
                        for dc in range(16):
                            pr.emit("tensor", lambda tt, kv=kv, dc=dc, ti=ti: tt.matmul(
                                pb[4][:], wv_bf[:, (kv * 16 + dc) * 128:(kv * 16 + dc + 1) * 128],
                                xn_grp[ti][:, dc * 512:(dc + 1) * 512],
                                start=(dc == 0), stop=(dc == 15), skip_group_check=True),
                                wait=[("ac", ac_k)] if dc == 0 else (),
                                inc=[("pe", 1)] if dc == 15 else ())
                        pe_v = pr.val("pe")
                        ac_v = pr.emit("scalar", lambda s: s.activation(
                            kv_sb[1][:], pb[4][:], AF.Identity),
                            wait=[("pe", pe_v), ("pe", kvsb1_free_pe)],
                            inc=[("ac", 1)])["ac"]
                        pb4_free_ac = ac_v
                        for k4 in range(4):
                            tile_idx = (0 if own else 16) + gl * 4 + k4
                            pe_tv = pr.emit("tensor", lambda tt, k4=k4: tt.transpose(
                                pbt[:, 0:128], kv_sb[1][:, _ts(k4)], ident_bf[:]),
                                wait=[("ac", ac_v), ("dv", pbt_free)],
                                inc=[("pe", 1)])["pe"]
                            kvsb1_free_pe = pe_tv
                            pbt_free = pr.emit("vector", lambda v, kv=kv, tile_idx=tile_idx:
                                               v.tensor_copy(
                                vtm[:, kv * S + tile_idx * 128: kv * S + (tile_idx + 1) * 128],
                                pbt[:, 0:128]),
                                wait=[("pe", pe_tv)], inc=[("dv", 1)])["dv"]
                    xng_free[ti] = pr.val("pe")
                    if own:
                        ws_sp = 0
                        for dc in range(16):
                            ws_sp = pr.emit("sync", lambda e, dc=dc, gl=gl, ti=ti: e.dma_start(
                                out=xn_hbm[dc, :, gl * 512:(gl + 1) * 512],
                                in_=xn_grp[ti][:, dc * 512:(dc + 1) * 512]),
                                wait=[("dv", grp_xn_dv)], inc=[("ws", 16)])["ws"]
                        xng_ws_free[ti] = ws_sp
                pr.fence()
                pr.flush(block, sems, waited)

            o_res = esAB.enter_context(nc.sbuf_tensor("o_res", [128, 16 * TOWN], BF16))
            esB = ExitStack()
            with esB:
                def sbb(name, shape, dtype):
                    return esB.enter_context(nc.sbuf_tensor(name, shape, dtype))

                xn_qb = [sbb(f"xn_qb{i}", [128, 16 * 512], BF16) for i in range(2)]
                wq_bf = [sbb(f"wq_bf{i}", [128, 2048], BF16) for i in range(2)]
                cq_t = sbb("cq_t", [128, 512], F32)
                sq_t = sbb("sq_t", [128, 512], F32)
                genB = [sbb(f"genB{i}", [128, 512], F32) for i in range(4)]
                masks = sbb("masks", [128, 8 * 512], BF16)
                posq = sbb("posq", [128, 512], F32)
                kofs = sbb("kofs", [128, 1], F32)
                q_sb = sbb("q_sb", [128, 512], BF16)
                qs_sb = sbb("qs_sb", [128, 512], BF16)
                qr_t1 = sbb("qr_t1", [128, 512], F32)
                qr_sb = sbb("qr_sb", [128, 512], BF16)
                pt_sb = [sbb(f"pt_sb{i}", [128, 512], BF16) for i in range(4)]
                rl_sb = sbb("rl_sb", [1, 512], F32)
                rlb_sb = sbb("rlb_sb", [128, 512], F32)

                pr = Prog(sched, on="B" in phases)
                pr.fence()
                sbanks = [pb[0], pb[1], pb[2], pb[3]]
                BAV, BL, BRB = pb[4], pb[5], pb[6]
                xnq_ready = [0, 0]
                xnq_free = [0, 0]
                wq_ready = [0, 0]
                wq_free = [0, 0]
                pt_free = [0, 0, 0, 0]
                sbank_free = [0, 0, 0, 0]
                av_free = 0
                l_free = 0
                rb_free = 0
                qr_free = 0
                pend_fin = None

                def emit_finalize_a(pe_done):
                    nonlocal l_free
                    dv_rl = pr.emit("vector", lambda v: v.reciprocal(rl_sb[:], BL[0:1, :]),
                                    wait=[("pe", pe_done)], inc=[("dv", 1)])["dv"]
                    l_free = dv_rl
                    return dv_rl

                def emit_finalize_b(h, qb_, dv_rl):
                    nonlocal av_free, rb_free
                    pe_rb = pr.emit("tensor", lambda tt: tt.matmul(
                        BRB[:], ones_row_f[:], rl_sb[:], start=True, stop=True,
                        skip_group_check=True),
                        wait=[("dv", dv_rl), ("dv", rb_free)], inc=[("pe", 1)])["pe"]
                    dv_rb = pr.emit("vector", lambda v: v.tensor_copy(rlb_sb[:], BRB[:]),
                                    wait=[("pe", pe_rb)], inc=[("dv", 1)])["dv"]
                    rb_free = dv_rb
                    dv_o = pr.emit("vector", lambda v, h=h, qb_=qb_: v.tensor_mul(
                        o_res[:, h * TOWN + qb_ * 512: h * TOWN + qb_ * 512 + 512],
                        BAV[:], rlb_sb[:]), inc=[("dv", 1)])["dv"]
                    av_free = dv_o

                def load_xnqb(qb, bi2):
                    ev = 0
                    for dc in range(16):
                        ev = pr.emit("sync", lambda e, dc=dc, qb=qb, bi2=bi2: e.dma_start(
                            out=xn_qb[bi2][:, dc * 512:(dc + 1) * 512],
                            in_=xn_hbm[dc, :, qb * 512:(qb + 1) * 512]),
                            wait=[("pe", xnq_free[bi2])], inc=[("ws", 16)])["ws"]
                    xnq_ready[bi2] = ev

                def load_wq(h, bi2):
                    wq_ready[bi2] = pr.emit("gpsimd", lambda g, h=h, bi2=bi2: g.dma_start(
                        out=wq_bf[bi2][:].rearrange("p (c j) -> p c j", c=16), in_=wq_r[h].rearrange("c p j -> p c j")),
                        wait=[("pe", wq_free[bi2])], inc=[("wg", 16)])["wg"]

                load_xnqb(0, 0)
                load_wq(0, 0)
                for qb in range(4):
                    bi = qb % 2
                    if qb < 3:
                        load_xnqb(qb + 1, (qb + 1) % 2)
                    tabq_dv = rope_tables(pr, PM_GO[qb], 512, cq_t, sq_t, genB, for_q=True)
                    pr.emit("gpsimd", lambda g: g.iota(
                        posq[:], pattern=[[1, 512]], base=0, channel_multiplier=0,
                        allow_small_or_imprecise_dtypes=True),
                        wait=[("dv", pr.val("dv"))], inc=[("go", 1)])
                    gv = pr.val("go")
                    pr.emit("vector", lambda v, qb=qb: v.tensor_scalar_add(
                        posq[:], posq[:], PM_GO[qb]), wait=[("go", gv)])
                    mask_dv = 0
                    for part in range(2):
                        basec = PM_GO[qb] if part == 0 else PM_GT[qb]
                        for kr in range(4):
                            mi = part * 4 + kr
                            mask_dv = _chain(pr, "vector", [
                                lambda v, basec=basec: v.tensor_add(
                                    kofs[:], basec, iota_p[:]),
                                lambda v, kr=kr: v.tensor_scalar_add(
                                    kofs[:], kofs[:], float(128 * kr)),
                                lambda v, mi=mi: v.tensor_scalar(
                                    masks[:, mi * 512:(mi + 1) * 512], posq[:], kofs[:], None,
                                    op0=ALU.is_ge),
                            ])
                    nch = 4 * (qb + 1)
                    for h in range(NH):
                        kv = h // 4
                        wbi = h % 2
                        if h < NH - 1:
                            load_wq(h + 1, (h + 1) % 2)
                        elif qb < 3:
                            load_wq(0, 0)
                        if pend_fin is not None:
                            fin_rl = emit_finalize_a(pend_fin[2])
                        for dc in range(16):
                            pr.emit("tensor", lambda tt, dc=dc, bi=bi, wbi=wbi: tt.matmul(
                                sbanks[2][:], wq_bf[wbi][:, _ts(dc)],
                                xn_qb[bi][:, dc * 512:(dc + 1) * 512],
                                start=(dc == 0), stop=(dc == 15), skip_group_check=True),
                                wait=([("wg", wq_ready[wbi]), ("ws", xnq_ready[bi]),
                                       ("ac", sbank_free[2])] if dc == 0 else ()),
                                inc=[("pe", 1)] if dc == 15 else ())
                        pe_q = pr.val("pe")
                        wq_free[wbi] = pe_q
                        ac_q = pr.emit("scalar", lambda s: s.activation(
                            q_sb[:], sbanks[2][:], AF.Identity),
                            wait=[("pe", pe_q), ("pe", qr_free)], inc=[("ac", 1)])["ac"]
                        sbank_free[2] = ac_q
                        if pend_fin is not None:
                            emit_finalize_b(pend_fin[0], pend_fin[1], fin_rl)
                            pend_fin = None
                        pe_sw = pr.emit("tensor", lambda tt: tt.matmul(
                            sbanks[3][:], swap_bf[:], q_sb[:], start=True, stop=True,
                            skip_group_check=True),
                            wait=[("ac", ac_q), ("ac", sbank_free[3])], inc=[("pe", 1)])["pe"]
                        ac_sw = pr.emit("scalar", lambda s: s.activation(
                            qs_sb[:], sbanks[3][:], AF.Identity),
                            wait=[("pe", pe_sw)], inc=[("ac", 1)])["ac"]
                        sbank_free[3] = ac_sw
                        pr.emit("vector", lambda v: v.tensor_mul(qr_t1[:], q_sb[:], cq_t[:]),
                                wait=[("ac", ac_sw), ("dv", tabq_dv), ("pe", qr_free)])
                        pr.emit("vector", lambda v: v.tensor_mul(qr_sb[:], qs_sb[:], sq_t[:]))
                        dv_qr = pr.emit("vector", lambda v: v.scalar_tensor_tensor(
                            qr_sb[:], qr_sb[:], 1.0, qr_t1[:], op0=ALU.mult, op1=ALU.add),
                            inc=[("dv", 1)])["dv"]
                        chunks = [(p_, kc) for p_ in range(2) for kc in range(nch)]
                        nchk = len(chunks)
                        exp_ev = [None] * nchk
                        pchunks = []

                        def emit_st(idx):
                            part, kc = chunks[idx]
                            col0 = kv * S + (0 if part == 0 else TOWN) + kc * 128
                            bk = idx % 4
                            pe_s = pr.emit("tensor", lambda tt, col0=col0, bk=bk: tt.matmul(
                                sbanks[bk][:], krope[:, col0:col0 + 128], qr_sb[:],
                                start=True, stop=True, skip_group_check=True),
                                wait=[("dv", dv_qr), ("ac", sbank_free[bk])],
                                inc=[("pe", 1)])["pe"]
                            pi = idx % 4
                            ac_e = pr.emit("scalar", lambda s, bk=bk, pi=pi: s.activation(
                                pt_sb[pi][:], sbanks[bk][:], AF.Exp, bias=B_NSH),
                                wait=[("pe", pe_s), ("pe", pt_free[pi])],
                                inc=[("ac", 1)])["ac"]
                            sbank_free[bk] = ac_e
                            if kc >= nch - 4:
                                mi = part * 4 + (kc - (nch - 4))
                                dv_m = pr.emit("vector", lambda v, pi=pi, mi=mi: v.tensor_mul(
                                    pt_sb[pi][:], pt_sb[pi][:],
                                    masks[:, mi * 512:(mi + 1) * 512]),
                                    wait=[("ac", ac_e), ("dv", mask_dv)],
                                    inc=[("dv", 1)])["dv"]
                                exp_ev[idx] = ("dv", dv_m)
                            else:
                                exp_ev[idx] = ("ac", ac_e)

                        def emit_pvl(idx):
                            part, kc = chunks[idx]
                            tile_idx = (0 if part == 0 else 16) + kc
                            pi = idx % 4
                            first = idx == 0
                            last = idx == nchk - 1
                            pr.emit("tensor", lambda tt, kv=kv, tile_idx=tile_idx, pi=pi,
                                    first=first, last=last: tt.matmul(
                                BAV[:],
                                vtm[:, kv * S + tile_idx * 128: kv * S + (tile_idx + 1) * 128],
                                pt_sb[pi][:], start=first, stop=last, skip_group_check=True),
                                wait=[exp_ev[idx]] + ([("dv", av_free)] if first else []),
                                inc=[("pe", 1)])
                            pe_l = pr.emit("tensor", lambda tt, pi=pi, first=first, last=last:
                                           tt.matmul(
                                BL[0:1, :], ones_col_bf[:], pt_sb[pi][:],
                                start=first, stop=last, skip_group_check=True),
                                wait=[("dv", l_free)] if first else (),
                                inc=[("pe", 1)])["pe"]
                            pt_free[pi] = pe_l

                        for idx in range(nchk):
                            emit_st(idx)
                            if idx >= 3:
                                emit_pvl(idx - 3)
                        emit_pvl(nchk - 3)
                        emit_pvl(nchk - 2)
                        emit_pvl(nchk - 1)
                        qr_free = pr.val("pe")
                        pend_fin = (h, qb, pr.val("pe"))
                    emit_finalize_b(pend_fin[0], pend_fin[1], emit_finalize_a(pend_fin[2]))
                    pend_fin = None
                    xnq_free[bi] = pr.val("pe")
                pr.fence()
                pr.flush(block, sems, waited)

                if "krope" in dbg_outs:
                    prd = Prog(sched)
                    prd.emit("sync", lambda e: e.dma_start(out=dbg_outs["krope"], in_=krope[:]),
                             inc=[("ws", 16)])
                    prd.emit("sync", lambda e: e.dma_start(out=dbg_outs["vtm"], in_=vtm[:]),
                             inc=[("ws", 16)])
                    prd.fence(("sync",))
                    prd.flush(block, sems, waited)

            esB2 = ExitStack()
            with esB2:
                wo_bf = [esB2.enter_context(nc.sbuf_tensor(f"wo_bf{i}", [128, 2048], BF16))
                         for i in range(2)]
                s_sb = [esB2.enter_context(nc.sbuf_tensor(f"s_sb{i}", [128, 512], BF16))
                        for i in range(2)]
                pr = Prog(sched, on="B2" in phases)
                pr.fence()
                wo_ready = [0, 0]
                wo_free = [0, 0]
                s_free = [0, 0]
                s_ws_free = [0, 0]

                def load_wo(od, wbi2):
                    wo_ready[wbi2] = pr.emit("gpsimd", lambda g, od=od, wbi2=wbi2: g.dma_start(
                        out=wo_bf[wbi2][:].rearrange("p (c j) -> p c j", c=16), in_=wo_r[od].rearrange("c p j -> p c j")),
                        wait=[("pe", wo_free[wbi2])], inc=[("wg", 16)])["wg"]

                load_wo(0, 0)
                load_wo(1, 1)
                for od in range(16):
                    wbi = od % 2
                    for tg in range(4):
                        bk = pb[tg % 2]
                        si = tg % 2
                        for dc in range(16):
                            pr.emit("tensor", lambda tt, wbi=wbi, dc=dc, tg=tg, bk=bk: tt.matmul(
                                bk[:], wo_bf[wbi][:, _ts(dc)],
                                o_res[:, dc * TOWN + tg * 512: dc * TOWN + tg * 512 + 512],
                                start=(dc == 0), stop=(dc == 15), skip_group_check=True),
                                wait=([("wg", wo_ready[wbi]), ("dv", s_free[si])]
                                      if dc == 0 else ()),
                                inc=[("pe", 1)] if dc == 15 else ())
                        pe_s = pr.val("pe")
                        if tg == 3:
                            wo_free[wbi] = pe_s
                            if od + 2 < 16:
                                load_wo(od + 2, wbi)
                        dv_s = pr.emit("vector", lambda v, si=si, bk=bk: v.tensor_copy(
                            s_sb[si][:], bk[:]),
                            wait=[("pe", pe_s), ("ws", s_ws_free[si])], inc=[("dv", 1)])["dv"]
                        ws_s = pr.emit("sync", lambda e, si=si, od=od, tg=tg: e.dma_start(
                            out=s_hbm[od, :, tg * 512:(tg + 1) * 512], in_=s_sb[si][:]),
                            wait=[("dv", dv_s)], inc=[("ws", 16)])["ws"]
                        s_free[si] = dv_s
                        s_ws_free[si] = ws_s
                pr.fence()
                pr.flush(block, sems, waited)
                if "o" in dbg_outs:
                    prd = Prog(sched)
                    prd.emit("sync", lambda e: e.dma_start(out=dbg_outs["o"], in_=o_res[:]),
                             inc=[("ws", 16)])
                    prd.fence(("sync",))
                    prd.flush(block, sems, waited)

        esC = ExitStack()
        with esC:
            def sbc(name, shape, dtype):
                return esC.enter_context(nc.sbuf_tensor(name, shape, dtype))

            xn_hf = sbc("xn_hf", [128, 16 * HALF], BF16)
            y_hf = sbc("y_hf", [128, 16 * HALF], F32)
            y_ready_dv = 0
            for hf in range(2):
                esCi = ExitStack()
                with esCi:
                    def sbi(name, shape, dtype):
                        return esCi.enter_context(nc.sbuf_tensor(name + str(hf), shape, dtype))

                    h_buf = sbi("h_buf", [128, 11 * HALF], BF16)
                    wgu_bf = [sbi(f"wgu_bf{i}", [128, 2048], BF16) for i in range(4)]
                    wd_full = sbi("wd_full", [128, 11 * 2048], BF16)
                    cb_sb = [sbi(f"cb_sb{i}", [128, HALF], F32) for i in range(2)]
                    crow = sbi("crow", [1, HALF], F32)
                    sg_sb = [sbi(f"sg_sb{i}", [128, 512], BF16) for i in range(2)]
                    uu_sb = [sbi(f"uu_sb{i}", [128, 512], BF16) for i in range(2)]

                    pr = Prog(sched, on="C" in phases)
                    pr.fence()
                    xn_ready = 0
                    for dc in range(16):
                        xn_ready = pr.emit("sync", lambda e, dc=dc, hf=hf: e.dma_start(
                            out=xn_hf[:, dc * HALF:(dc + 1) * HALF],
                            in_=xn_hbm[dc, :, hf * HALF:(hf + 1) * HALF]),
                            inc=[("ws", 16)])["ws"]

                    wgu_ready = [0, 0, 0, 0]
                    wgu_free = [0, 0, 0, 0]
                    wd_ready = 0
                    wd_free = 0
                    cb_ready = [0, 0]
                    crow_free = 0
                    pb6_free = 0
                    g_bank_free = [0, 0]
                    u_bank_free = [0, 0]
                    d_bank_free = [0, 0]
                    sg_free = [0, 0]
                    h_read_pe = 0
                    gu_slot = [0]

                    def load_wgu(e_i, jc):
                        wbi = gu_slot[0] % 4
                        gu_slot[0] += 1
                        src = wgu_r[e_i, jc] if e_i < EXPERTS else sgu_r[jc]
                        wgu_ready[wbi] = pr.emit("gpsimd", lambda g, src=src, wbi=wbi: g.dma_start(
                            out=wgu_bf[wbi][:].rearrange("p (c j) -> p c j", c=16), in_=src.rearrange("c p j -> p c j")),
                            wait=[("pe", wgu_free[wbi])], inc=[("wg", 16)])["wg"]
                        return wbi

                    def load_wd(e_i, hpass):
                        nonlocal wd_ready
                        for ic in range(11):
                            src = (wd_r[e_i, ic] if e_i < EXPERTS
                                   else sdn_r[hpass * 11 + ic])
                            wd_ready = pr.emit("gpsimd", lambda g, src=src, ic=ic: g.dma_start(
                                out=wd_full[:, ic * 2048:(ic + 1) * 2048], in_=src),
                                wait=[("pe", wd_free)] if ic == 0 else (),
                                inc=[("wg", 16)])["wg"]

                    def load_cb(e_i, cbi):
                        nonlocal crow_free, pb6_free
                        ws_c = pr.emit("sync", lambda e, e_i=e_i, hf=hf: e.dma_start(
                            out=crow[:], in_=comb_hbm[e_i:e_i + 1, hf * HALF:(hf + 1) * HALF]),
                            wait=[("pe", crow_free)], inc=[("ws", 16)])["ws"]
                        for h2 in range(2):
                            pe_b = pr.emit("tensor", lambda tt, h2=h2: tt.matmul(
                                pb[6][:], ones_row_f[:], crow[0:1, h2 * 512:(h2 + 1) * 512],
                                start=True, stop=True, skip_group_check=True),
                                wait=[("ws", ws_c), ("dv", pb6_free)], inc=[("pe", 1)])["pe"]
                            crow_free = pe_b
                            pb6_free = pr.emit("vector", lambda v, cbi=cbi, h2=h2: v.tensor_copy(
                                cb_sb[cbi][:, h2 * 512:(h2 + 1) * 512], pb[6][:]),
                                wait=[("pe", pe_b)], inc=[("dv", 1)])["dv"]
                        cb_ready[cbi] = pb6_free

                    units = [(e, 0) for e in range(EXPERTS)] + [(EXPERTS, 0), (EXPERTS, 1)]
                    if C_UNITS is not None:
                        units = list(C_UNITS)
                    e0, hp0 = units[0]
                    if e0 < EXPERTS:
                        load_cb(e0, e0 % 2)
                    load_wgu(e0, hp0 * 11 if e0 == EXPERTS else 0)
                    for ui, (e_i, hpass) in enumerate(units):
                        cbi = e_i % 2
                        h_dv = 0
                        for c in range(11):
                            for pw in range(2):
                                if e_i == EXPERTS:
                                    jc = (hpass * 11 + c) if pw == 0 else (22 + hpass * 11 + c)
                                else:
                                    jc = c if pw == 0 else (11 + c)
                                wbi = (gu_slot[0] - 1) % 4 if (ui == 0 and c == 0 and pw == 0) \
                                    else load_wgu(e_i, jc)
                                if ui == 0 and c == 0 and pw == 0:
                                    wbi = 0
                                myrdy = wgu_ready[wbi]
                                for tg in range(2):
                                    bk = (pb[0], pb[1])[tg] if pw == 0 else (pb[2], pb[3])[tg]
                                    bfree = ([("ac", g_bank_free[tg])] if pw == 0
                                             else [("dv", u_bank_free[tg])])
                                    for dc in range(16):
                                        pr.emit("tensor", lambda tt, dc=dc, tg=tg, wbi=wbi, bk=bk:
                                                tt.matmul(
                                            bk[:], wgu_bf[wbi][:, _ts(dc)],
                                            xn_hf[:, dc * HALF + tg * 512: dc * HALF + tg * 512 + 512],
                                            start=(dc == 0), stop=(dc == 15),
                                            skip_group_check=True),
                                            wait=([("wg", myrdy), ("ws", xn_ready)] + bfree
                                                  if dc == 0 else ()),
                                            inc=[("pe", 1)] if dc == 15 else ())
                                    pe_gu = pr.val("pe")
                                    if tg == 1:
                                        wgu_free[wbi] = pe_gu
                                    if pw == 0:
                                        g_bank_free[tg] = pr.emit(
                                            "scalar", lambda s, tg=tg, bk=bk: s.activation(
                                                sg_sb[tg][:], bk[:], AF.Silu),
                                            wait=[("pe", pe_gu), ("dv", sg_free[tg])],
                                            inc=[("ac", 1)])["ac"]
                                    else:
                                        dv_u = pr.emit("vector", lambda v, tg=tg, bk=bk:
                                                       v.tensor_copy(uu_sb[tg][:], bk[:]),
                                                       wait=[("pe", pe_gu)],
                                                       inc=[("dv", 1)])["dv"]
                                        u_bank_free[tg] = dv_u
                                        col = c * HALF + tg * 512
                                        h_dv = pr.emit("vector", lambda v, tg=tg, col=col:
                                                       v.tensor_mul(
                                            h_buf[:, col:col + 512], sg_sb[tg][:], uu_sb[tg][:]),
                                            wait=[("ac", g_bank_free[tg]),
                                                  ("dv", cb_ready[cbi]),
                                                  ("pe", h_read_pe)],
                                            inc=[("dv", 1)])["dv"]
                                        if e_i < EXPERTS:
                                            h_dv = pr.emit("vector", lambda v, tg=tg, col=col,
                                                           cbi=cbi: v.tensor_mul(
                                                h_buf[:, col:col + 512],
                                                h_buf[:, col:col + 512],
                                                cb_sb[cbi][:, tg * 512:(tg + 1) * 512]),
                                                inc=[("dv", 1)])["dv"]
                                        sg_free[tg] = h_dv
                        if e_i + 1 <= EXPERTS and hpass == 0 and e_i + 1 < EXPERTS:
                            load_cb(e_i + 1, (e_i + 1) % 2)
                        load_wd(e_i, hpass)
                        for od in range(16):
                            for tg in range(2):
                                dbk = (od * 2 + tg) % 2
                                bk = pb[4 + dbk]
                                for ic in range(11):
                                    pr.emit("tensor", lambda tt, ic=ic, od=od, tg=tg, bk=bk:
                                            tt.matmul(
                                        bk[:], wd_full[:, ic * 2048 + od * 128: ic * 2048 + (od + 1) * 128],
                                        h_buf[:, ic * HALF + tg * 512: ic * HALF + tg * 512 + 512],
                                        start=(ic == 0), stop=(ic == 10),
                                        skip_group_check=True),
                                        wait=([("wg", wd_ready), ("dv", h_dv),
                                               ("dv", d_bank_free[dbk])] if ic == 0 else ()),
                                        inc=[("pe", 1)] if ic == 10 else ())
                                pe_d = pr.val("pe")
                                ysl = y_hf[:, od * HALF + tg * 512: od * HALF + tg * 512 + 512]
                                if ui == 0:
                                    dv_y = pr.emit("vector", lambda v, ysl=ysl, bk=bk:
                                                   v.tensor_copy(ysl, bk[:]),
                                                   wait=[("pe", pe_d)], inc=[("dv", 1)])["dv"]
                                else:
                                    dv_y = pr.emit("vector", lambda v, ysl=ysl, bk=bk:
                                                   v.tensor_add(ysl, bk[:], ysl),
                                                   wait=[("pe", pe_d)], inc=[("dv", 1)])["dv"]
                                d_bank_free[dbk] = dv_y
                                y_ready_dv = dv_y
                        h_read_pe = pr.val("pe")
                        wd_free = h_read_pe
                    pr.fence()
                    pr.flush(block, sems, waited)
                    if ("y%d" % hf) in dbg_outs:
                        prd = Prog(sched)
                        prd.emit("sync", lambda e, hf=hf: e.dma_start(
                            out=dbg_outs["y%d" % hf], in_=y_hf[:]), inc=[("ws", 16)])
                        prd.fence(("sync",))
                        prd.flush(block, sems, waited)

                esD = ExitStack()
                with esD:
                    def sbd(name, shape, dtype):
                        return esD.enter_context(nc.sbuf_tensor(name + str(hf), shape, dtype))

                    ytm = [sbd(f"ytm{i}", [128, D], F32) for i in range(3)]
                    stm = [sbd(f"stm{i}", [128, D], BF16) for i in range(3)]
                    gtm = [sbd(f"gtm{i}", [128, D], F32) for i in range(3)]
                    xtd = [sbd(f"xtd{i}", [128, D], F32) for i in range(3)]

                    prd = Prog(sched, on="D" in phases)
                    prd.fence()
                    stm_ready = [0, 0, 0]
                    ytm_free = [0, 0, 0]
                    stm_free = [0, 0, 0]
                    gtm_free = [0, 0, 0]
                    xtd_free = [0, 0, 0]
                    ytp_free = [(0, 0), (0, 0), (0, 0), (0, 0)]
                    for tt_i in range(8):
                        bi = tt_i % 3
                        row0 = hf * HALF + tt_i * 128
                        x_rdy = prd.emit("sync", lambda e, bi=bi, row0=row0: e.dma_start(
                            out=xtd[bi][:], in_=x_own[row0:row0 + 128, :]),
                            wait=[("ws", xtd_free[bi])], inc=[("ws", 16)])["ws"]
                        for dc in range(16):
                            stm_ready[bi] = prd.emit("sync", lambda e, bi=bi, dc=dc, tt_i=tt_i,
                                                     hf=hf: e.dma_start(
                                out=stm[bi][:, _ts(dc)],
                                in_=s_hbm[dc, :, hf * HALF + tt_i * 128:
                                          hf * HALF + tt_i * 128 + 128],
                                transpose=True),
                                wait=[("dv", stm_free[bi])] if dc == 0 else (),
                                inc=[("ws", 16)])["ws"]
                        for dc in range(16):
                            pk = dc % 4
                            pe_y = prd.emit("tensor", lambda tt, dc=dc, tt_i=tt_i, pk=pk:
                                            tt.transpose(
                                pb[pk][:, 0:128],
                                y_hf[:, dc * HALF + tt_i * 128: dc * HALF + tt_i * 128 + 128],
                                ident_f[:]),
                                wait=[("dv", y_ready_dv), ("dv", ytp_free[pk][0])],
                                inc=[("pe", 1)])["pe"]
                            dv_ye = prd.emit("vector", lambda v, bi=bi, dc=dc, pk=pk:
                                             v.tensor_copy(ytm[bi][:, _ts(dc)], pb[pk][:, 0:128]),
                                             wait=[("pe", pe_y), ("ac", ytm_free[bi])],
                                             inc=[("dv", 1)])["dv"]
                            ytp_free[pk] = (dv_ye, 0)
                        dv_tp = prd.val("dv")
                        ac_g = prd.emit("scalar", lambda s, bi=bi: s.activation(
                            gtm[bi][:], ytm[bi][:], AF.Silu),
                            wait=[("dv", dv_tp), ("dv", gtm_free[bi])], inc=[("ac", 1)])["ac"]
                        ytm_free[bi] = ac_g
                        prd.emit("vector", lambda v, bi=bi: v.tensor_mul(
                            gtm[bi][:], stm[bi][:], gtm[bi][:]),
                            wait=[("ac", ac_g), ("ws", x_rdy), ("ws", stm_ready[bi])])
                        dv_out = prd.emit("vector", lambda v, bi=bi: v.tensor_add(
                            xtd[bi][:], gtm[bi][:], xtd[bi][:]), inc=[("dv", 1)])["dv"]
                        stm_free[bi] = dv_out
                        gtm_free[bi] = dv_out
                        ws_out = prd.emit("sync", lambda e, bi=bi, row0=row0: e.dma_start(
                            out=out_t[row0:row0 + 128, :], in_=xtd[bi][:]),
                            wait=[("dv", dv_out)], inc=[("ws", 16)])["ws"]
                        xtd_free[bi] = ws_out
                    prd.fence()
                    prd.flush(block, sems, waited)

        def _final(e):
            e.wait_ge(sems["ws"], sched.get("ws", 0))
            e.wait_ge(sems["wg"], sched.get("wg", 0))
        block.sync(_final)

    return nc


def _prep_shared(Wq, Wk, Wv, Wo, gate_w, w_gate_up, w_down, sgu, sdn):
    d = {}
    d["wq_r"] = np.ascontiguousarray(Wq.reshape(16, 128, NH, 128).transpose(2, 0, 1, 3))
    d["wk_r"] = np.ascontiguousarray(Wk.reshape(16, 128, NKV, 128).transpose(2, 0, 1, 3))
    d["wv_r"] = np.ascontiguousarray(Wv.reshape(16, 128, NKV, 128).transpose(2, 0, 1, 3))
    d["wo_r"] = np.ascontiguousarray(Wo.reshape(16, 128, 16, 128).transpose(2, 0, 1, 3))
    d["gw_r"] = np.ascontiguousarray(gate_w.T.reshape(16, 128, 16))
    d["wgu_r"] = np.ascontiguousarray(
        w_gate_up.reshape(EXPERTS, 16, 128, 22, 128).transpose(0, 3, 1, 2, 4))
    d["wd_r"] = np.ascontiguousarray(w_down.reshape(EXPERTS, 11, 128, D))
    d["sgu_r"] = np.ascontiguousarray(sgu.reshape(16, 128, 44, 128).transpose(2, 0, 1, 3))
    d["sdn_r"] = np.ascontiguousarray(sdn.reshape(22, 128, D))
    return d


def _core_inputs(x, b, m, shared):
    own_rows = np.concatenate([np.arange(512 * (2 * j + m), 512 * (2 * j + m) + 512)
                               for j in range(4)])
    oth_rows = np.concatenate([np.arange(512 * (2 * j + 1 - m), 512 * (2 * j + 1 - m) + 512)
                               for j in range(4)])
    d = dict(shared)
    d["x_own"] = np.ascontiguousarray(x[b][own_rows])
    d["x_oth"] = np.ascontiguousarray(x[b][oth_rows])
    pm = np.zeros((128, 16), dtype=np.float32)
    for j in range(4):
        pm[:, j] = 512 * (2 * j + m)
        pm[:, 4 + j] = 512 * (2 * j + 1 - m)
    d["posmeta"] = pm
    return d, own_rows


def kernel(L0_attn_x, L0_attn_Wq, L0_attn_Wk, L0_attn_Wv, L0_attn_Wo,
           L0_ffn_gate_w, L0_ffn_w_gate_up, L0_ffn_w_down,
           L0_ffn_shared_gate_up, L0_ffn_shared_down):
    x = np.asarray(L0_attn_x, dtype=np.float32)
    shared = _prep_shared(np.asarray(L0_attn_Wq), np.asarray(L0_attn_Wk),
                          np.asarray(L0_attn_Wv), np.asarray(L0_attn_Wo),
                          np.asarray(L0_ffn_gate_w), np.asarray(L0_ffn_w_gate_up),
                          np.asarray(L0_ffn_w_down), np.asarray(L0_ffn_shared_gate_up),
                          np.asarray(L0_ffn_shared_down))
    in_maps = []
    row_maps = []
    for c in range(8):
        d, own_rows = _core_inputs(x, c // 2, c % 2, shared)
        in_maps.append(d)
        row_maps.append(own_rows)
    nc = build_program()
    res = run_bass_kernel_spmd(nc, in_maps, core_ids=list(range(8)))
    out = np.empty((B, S, D), dtype=np.float32)
    for c in range(8):
        out[c // 2][row_maps[c]] = res.results[c]["out"]
    return out


# version 12
# speedup vs baseline: 1.1257x; 1.1257x over previous
import numpy as np
import concourse.bass as bass
import concourse.mybir as mybir
from concourse.bass_utils import run_bass_kernel_spmd
from contextlib import ExitStack

F32 = mybir.dt.float32
BF16 = mybir.dt.bfloat16
AF = mybir.ActivationFunctionType
ALU = mybir.AluOpType
AX = mybir.AxisListType

B, S, D = 4, 4096, 2048
HD, NH, NKV = 128, 16, 4
EXPERTS = 16
THETA = 10000.0
SCALE_BASE = 512.0
LN_EPS = 1e-5
EXP_SHIFT = 30.0
MAGIC = 12582912.0
TOWN = 2048
HALF = 1024
INV2PI = 1.0 / (2.0 * np.pi)

ENGINES = ("tensor", "vector", "scalar", "gpsimd", "sync")
C_UNITS = None


class Prog:

    def __init__(self, sched, on=True):
        self.q = {e: [] for e in ENGINES}
        self.sched = sched
        self.on = on

    def val(self, sem):
        return self.sched.get(sem, 0)

    def emit(self, eng, fn, wait=(), inc=()):
        if not self.on:
            return {sname: 0 for sname, _ in inc}
        out = {}
        incs = []
        for sname, n in inc:
            v = self.sched.get(sname, 0) + n
            self.sched[sname] = v
            out[sname] = v
            incs.append((sname, n))
        wait = tuple((sn, vv) for sn, vv in wait if vv and vv > 0)
        self.q[eng].append((fn, wait, tuple(incs)))
        return out

    def fence(self, engines=ENGINES):
        snap = [(sn, v) for sn, v in self.sched.items() if v > 0]
        for e in engines:
            self.emit(e, lambda x: x.nop(), wait=snap)

    def flush(self, block, sems, waited):
        def runner(ename):
            def run(engobj):
                w = waited[ename]
                for fn, waits, incs in self.q[ename]:
                    for sname, v in waits:
                        if v > w.get(sname, 0):
                            engobj.wait_ge(sems[sname], v)
                            w[sname] = v
                    inst = fn(engobj)
                    for sname, n in incs:
                        inst = inst.then_inc(sems[sname], n)
            return run

        for e in ENGINES:
            if self.q[e]:
                getattr(block, e)(runner(e))
        self.q = {e: [] for e in ENGINES}


def _ts(i, w=128):
    return bass.ts(i, w)


def build_program(dbg=(), phases=("A", "B", "B2", "C", "D")):
    nc = bass.Bass("TRN2", target_bir_lowering=False)
    dbg = set(dbg)
    phases = set(phases)

    x_own = nc.dram_tensor("x_own", [TOWN, D], F32, kind="ExternalInput").ap()
    x_oth = nc.dram_tensor("x_oth", [TOWN, D], F32, kind="ExternalInput").ap()
    posmeta = nc.dram_tensor("posmeta", [128, 16], F32, kind="ExternalInput").ap()
    gw_r = nc.dram_tensor("gw_r", [16, 128, 16], F32, kind="ExternalInput").ap()
    wq_r = nc.dram_tensor("wq_r", [NH, 16, 128, 128], F32, kind="ExternalInput").ap()
    wk_r = nc.dram_tensor("wk_r", [NKV, 16, 128, 128], F32, kind="ExternalInput").ap()
    wv_r = nc.dram_tensor("wv_r", [NKV, 16, 128, 128], F32, kind="ExternalInput").ap()
    wo_r = nc.dram_tensor("wo_r", [16, 16, 128, 128], F32, kind="ExternalInput").ap()
    wgu_r = nc.dram_tensor("wgu_r", [EXPERTS, 22, 16, 128, 128], F32, kind="ExternalInput").ap()
    wd_r = nc.dram_tensor("wd_r", [EXPERTS, 11, 128, D], F32, kind="ExternalInput").ap()
    sgu_r = nc.dram_tensor("sgu_r", [44, 16, 128, 128], F32, kind="ExternalInput").ap()
    sdn_r = nc.dram_tensor("sdn_r", [22, 128, D], F32, kind="ExternalInput").ap()
    out_t = nc.dram_tensor("out", [TOWN, D], F32, kind="ExternalOutput").ap()

    def dscratch(name, shape, dtype, key):
        kind = "ExternalOutput" if key in dbg else "Internal"
        return nc.dram_tensor(name, shape, dtype, kind=kind).ap()

    xn_hbm = dscratch("xn_hbm", [16, 128, TOWN], BF16, "xn")
    s_hbm = dscratch("s_hbm", [16, 128, TOWN], BF16, "s")
    comb_hbm = dscratch("comb_hbm", [16, TOWN], F32, "comb")
    dbg_outs = {}
    for key, shape, dt in [("tab", [128, 2048], F32),
                           ("krope", [128, NKV * S], BF16), ("vtm", [128, NKV * S], BF16),
                           ("o", [128, 16 * TOWN], BF16), ("y0", [128, 16 * HALF], F32),
                           ("y1", [128, 16 * HALF], F32)]:
        if key in dbg:
            dbg_outs[key] = nc.dram_tensor("dbg_" + key, shape, dt, kind="ExternalOutput").ap()

    sched = {}
    sems = {}
    waited = {e: {} for e in ENGINES}

    es = ExitStack()
    with es:
        for sname in ("ws", "wg", "pe", "dv", "ac", "go"):
            sems[sname] = es.enter_context(nc.semaphore(sname))
        block = es.enter_context(nc.Block())

        def sbg(name, shape, dtype):
            return es.enter_context(nc.sbuf_tensor(name, shape, dtype))

        ident_bf = sbg("ident_bf", [128, 128], BF16)
        ident_f = sbg("ident_f", [128, 128], F32)
        swap_bf = sbg("swap_bf", [128, 128], BF16)
        ones_col_bf = sbg("ones_col_bf", [128, 1], BF16)
        ones_row_f = sbg("ones_row_f", [1, 128], F32)
        iota_p = sbg("iota_p", [128, 1], F32)
        cols_c = sbg("cols_c", [128, 12], F32)
        pm_sb = sbg("pm_sb", [128, 16], F32)
        gw_sb = sbg("gw_sb", [128, 256], F32)
        pb = [es.enter_context(nc.psum_tensor(f"pb{i}", [128, 512], F32)) for i in range(7)]
        pbt = es.enter_context(nc.psum_tensor("pbt", [128, 1024], BF16))

        PARC = cols_c[:, 0:1]
        NPAR = cols_c[:, 1:2]
        SIGN = cols_c[:, 2:3]
        INVF = cols_c[:, 3:4]
        TMPC = cols_c[:, 4:5]
        TMPC2 = cols_c[:, 5:6]
        B_L2PI = cols_c[:, 6:7]
        B_QX = cols_c[:, 7:8]
        B_EPS = cols_c[:, 8:9]
        B_NSH = cols_c[:, 9:10]

        pr = Prog(sched)
        pr.emit("sync", lambda e: e.dma_start(out=pm_sb[:], in_=posmeta), inc=[("ws", 16)])
        pr.emit("sync", lambda e: e.dma_start(out=gw_sb[:].rearrange("p (c e) -> p c e", c=16), in_=gw_r.rearrange("c p e -> p c e")),
                inc=[("ws", 16)])
        ws_consts = pr.val("ws")

        def _init_g(g):
            n = [0]

            def step(fn):
                fn().then_inc(sems["go"], 1)
                n[0] += 1
                g.wait_ge(sems["go"], n[0])
            step(lambda: g.memset(ident_bf[:], 1.0))
            step(lambda: g.affine_select(ident_bf[:], ident_bf[:], pattern=[[-1, 128]],
                                         compare_op=ALU.is_equal, fill=0.0, base=0,
                                         channel_multiplier=1))
            step(lambda: g.memset(ident_f[:], 1.0))
            step(lambda: g.affine_select(ident_f[:], ident_f[:], pattern=[[-1, 128]],
                                         compare_op=ALU.is_equal, fill=0.0, base=0,
                                         channel_multiplier=1))
            step(lambda: g.memset(swap_bf[:], 1.0))
            step(lambda: g.affine_select(swap_bf[:, 0:64], swap_bf[:, 0:64], pattern=[[-1, 64]],
                                         compare_op=ALU.is_equal, fill=0.0, base=-64,
                                         channel_multiplier=1))
            step(lambda: g.affine_select(swap_bf[:, 64:128], swap_bf[:, 64:128],
                                         pattern=[[-1, 64]], compare_op=ALU.is_equal,
                                         fill=0.0, base=0, channel_multiplier=1))
            step(lambda: g.memset(ones_col_bf[:], 1.0))
            step(lambda: g.memset(ones_row_f[:], 1.0))
            step(lambda: g.memset(B_L2PI, float(np.log(INV2PI))))
            step(lambda: g.memset(B_QX, float(-0.5 * np.log(HD))))
            step(lambda: g.memset(B_EPS, LN_EPS))
            step(lambda: g.memset(B_NSH, -EXP_SHIFT))
            step(lambda: g.iota(iota_p[:], pattern=[[0, 1]], base=0, channel_multiplier=1,
                                allow_small_or_imprecise_dtypes=True))
            return g.drain()
        sched["go"] = sched.get("go", 0) + 14
        go_init = pr.emit("gpsimd", _init_g, inc=[("go", 1)])["go"]

        def _chain(pr, eng, fns, wait0=()):
            last = 0
            for i, fn in enumerate(fns):
                last = pr.emit(eng, fn, wait=(list(wait0) if i == 0 else [])
                               + [("dv" if eng == "vector" else "ac", last)],
                               inc=[("dv" if eng == "vector" else "ac", 1)])[
                                   "dv" if eng == "vector" else "ac"]
            return last

        dv_init = _chain(pr, "vector", [
            lambda v: v.tensor_scalar(TMPC, iota_p[:], 0.5, -0.25, op0=ALU.mult, op1=ALU.add),
            lambda v: v.tensor_scalar_add(TMPC, TMPC, MAGIC),
            lambda v: v.tensor_scalar_sub(TMPC, TMPC, MAGIC),
            lambda v: v.scalar_tensor_tensor(PARC, TMPC, 2.0, iota_p[:],
                                             op0=ALU.mult, op1=ALU.subtract),
            lambda v: v.tensor_scalar(PARC, PARC, 2.0, 1.0, op0=ALU.mult, op1=ALU.add),
            lambda v: v.tensor_scalar_mul(NPAR, PARC, -1.0),
            lambda v: v.tensor_scalar(SIGN, iota_p[:], 64.0, None, op0=ALU.is_ge),
            lambda v: v.tensor_scalar(SIGN, SIGN, 2.0, -1.0, op0=ALU.mult, op1=ALU.add),
            lambda v: v.tensor_scalar(TMPC, iota_p[:], 64.0, None, op0=ALU.is_ge),
            lambda v: v.scalar_tensor_tensor(TMPC, TMPC, 64.0, iota_p[:],
                                             op0=ALU.mult, op1=ALU.subtract),
            lambda v: v.tensor_scalar_mul(TMPC, TMPC, np.log(THETA) / 64.0),
        ], wait0=[("go", go_init)])
        pr.emit("scalar", lambda s: s.activation(INVF, TMPC, AF.Exp, bias=B_L2PI),
                wait=[("dv", dv_init)], inc=[("ac", 1)])
        pr.fence()
        pr.flush(block, sems, waited)

        PM_GO = [pm_sb[:, j:j + 1] for j in range(4)]
        PM_GT = [pm_sb[:, 4 + j:5 + j] for j in range(4)]

        def rope_tables(pr, base_col, w, cosd, sind, gen, for_q):
            A, Bv, C, Dv = (g[:] for g in gen)
            cosd = cosd[:]
            sind = sind[:]
            pr.emit("gpsimd", lambda g: g.iota(A, pattern=[[1, w]], base=0, channel_multiplier=0,
                                               allow_small_or_imprecise_dtypes=True),
                    wait=[("go", go_init), ("dv", pr.val("dv")), ("ac", pr.val("ac"))],
                    inc=[("go", 1)])
            gv = pr.val("go")
            pr.emit("vector", lambda v: v.tensor_scalar_add(A, A, base_col),
                    wait=[("go", gv), ("ws", ws_consts)])
            pr.emit("vector", lambda v: v.tensor_scalar_mul(Bv, A, INVF))
            pr.emit("vector", lambda v: v.tensor_scalar_add(C, Bv, MAGIC))
            pr.emit("vector", lambda v: v.tensor_scalar_sub(C, C, MAGIC))
            pr.emit("vector", lambda v: v.tensor_sub(C, Bv, C))
            dvv = pr.emit("vector", lambda v: v.tensor_scalar_mul(C, C, 2.0 * np.pi),
                          inc=[("dv", 1)])["dv"]
            ac_s = pr.emit("scalar", lambda s: s.activation(sind, C, AF.Sin),
                           wait=[("dv", dvv)], inc=[("ac", 1)])["ac"]
            pr.emit("vector", lambda v: v.tensor_scalar_add(Dv, Bv, 0.25),
                    wait=[("ac", ac_s)])
            pr.emit("vector", lambda v: v.tensor_scalar_add(C, Dv, MAGIC))
            pr.emit("vector", lambda v: v.tensor_scalar_sub(C, C, MAGIC))
            pr.emit("vector", lambda v: v.tensor_sub(C, Dv, C))
            dvv = pr.emit("vector", lambda v: v.tensor_scalar_mul(C, C, 2.0 * np.pi),
                          inc=[("dv", 1)])["dv"]
            ac_c = pr.emit("scalar", lambda s: s.activation(cosd, C, AF.Sin),
                           wait=[("dv", dvv)], inc=[("ac", 1)])["ac"]
            dvv = pr.emit("vector", lambda v: v.tensor_scalar(
                Bv, A, 1.0 / SCALE_BASE, 0.5, op0=ALU.mult, op1=ALU.add),
                wait=[("ac", ac_c)], inc=[("dv", 1)])["dv"]
            ac_l = pr.emit("scalar", lambda s: s.activation(Bv, Bv, AF.Ln),
                           wait=[("dv", dvv)], inc=[("ac", 1)])["ac"]
            par = PARC if for_q else NPAR
            dvv = pr.emit("vector", lambda v: v.tensor_scalar_mul(A, Bv, par),
                          wait=[("ac", ac_l)], inc=[("dv", 1)])["dv"]
            if for_q:
                ac_x = pr.emit("scalar", lambda s: s.activation(A, A, AF.Exp, bias=B_QX),
                               wait=[("dv", dvv)], inc=[("ac", 1)])["ac"]
            else:
                ac_x = pr.emit("scalar", lambda s: s.activation(A, A, AF.Exp),
                               wait=[("dv", dvv)], inc=[("ac", 1)])["ac"]
            pr.emit("vector", lambda v: v.tensor_mul(cosd, cosd, A), wait=[("ac", ac_x)])
            pr.emit("vector", lambda v: v.tensor_scalar_mul(A, A, SIGN))
            return pr.emit("vector", lambda v: v.tensor_mul(sind, sind, A),
                           inc=[("dv", 1)])["dv"]

        esAB = ExitStack()
        with esAB:
            krope = esAB.enter_context(nc.sbuf_tensor("krope", [128, NKV * S], BF16))
            vtm = esAB.enter_context(nc.sbuf_tensor("vtm", [128, NKV * S], BF16))

            esA = ExitStack()
            with esA if "A" in phases else ExitStack():
              if "A" in phases:
                def sba(name, shape, dtype):
                    return esA.enter_context(nc.sbuf_tensor(name, shape, dtype))

                wk_bf = sba("wk_bf", [128, NKV * 16 * 128], BF16)
                wv_bf = sba("wv_bf", [128, NKV * 16 * 128], BF16)
                xt = [sba(f"xt{i}", [128, D], F32) for i in range(3)]
                lnst = sba("lnst", [128, 48], F32)
                lncol = sba("lncol", [128, 16], F32)
                xn_tm = [sba(f"xn_tm{i}", [128, D], F32) for i in range(2)]
                xn_grp = [sba(f"xn_grp{i}", [128, 16 * 512], BF16) for i in range(2)]
                xnf32 = sba("xnf32", [128, 16 * 128], F32)
                tabs = [(sba(f"ckt{i}", [128, 512], F32), sba(f"skt{i}", [128, 512], F32))
                        for i in range(2)]
                genS = [sba(f"genS{i}", [128, 512], F32) for i in range(4)]
                kv_sb = [sba(f"kv_sb{i}", [128, 512], BF16) for i in range(2)]
                kv_sw = sba("kv_sw", [128, 512], BF16)
                kv_t1 = sba("kv_t1", [128, 512], F32)
                kv_t2 = sba("kv_t2", [128, 512], F32)
                lg_sb = sba("lg_sb", [16, 128], F32)
                lt_sb = sba("lt_sb", [128, 16], F32)
                gcol = sba("gcol", [128, 8], F32)
                gm_sb = sba("gm_sb", [128, 16], F32)

                pr = Prog(sched, on="A" in phases)
                pr.emit("gpsimd", lambda g: g.dma_start(
                    out=wk_bf[:].rearrange("p (k c j) -> p k c j", k=NKV, c=16), in_=wk_r.rearrange("k c p j -> p k c j")),
                    wait=[("go", go_init)], inc=[("wg", 16)])
                pr.emit("gpsimd", lambda g: g.dma_start(
                    out=wv_bf[:].rearrange("p (k c j) -> p k c j", k=NKV, c=16), in_=wv_r.rearrange("k c p j -> p k c j")),
                    inc=[("wg", 16)])
                wg_kv = pr.val("wg")

                xt_free = [(0, 0), (0, 0), (0, 0)]
                xtm_free = [0, 0]
                xng_free = [0, 0]
                xng_ws_free = [0, 0]
                tp_free = [(0, 0), (0, 0)]
                ci_free = [(0, 0)] * 4
                glog_free = 0
                gtp_free = 0
                xnf32_free = 0
                pb4_free_ac = 0
                kvsb0_free_pe = 0
                kvsb0_free_dv = 0
                kvsw_free_dv = 0
                kvsb1_free_pe = 0
                pbt_free = 0
                tab_ready = [0, 0]

                ln_state = {}

                def emit_ln(t, src_t, row16):
                    bx = t % 3
                    ci = t % 4
                    CS = lncol[:, 4 * ci:4 * ci + 4]
                    c_mean, c_var, c_rstd = (CS[:, 0:1], CS[:, 1:2], CS[:, 3:4])
                    row0 = row16 * 128
                    x_rdy = pr.emit("sync", lambda e, bx=bx, src_t=src_t, row0=row0: e.dma_start(
                        out=xt[bx][:], in_=src_t[row0:row0 + 128, :]),
                        wait=[("dv", xt_free[bx][0]), ("ac", xt_free[bx][1])],
                        inc=[("ws", 16)])["ws"]
                    bnb = lnst[:, 24 * (t % 2):24 * (t % 2) + 24]
                    for j in range(4):
                        pr.emit("vector", lambda v, bx=bx, j=j, bnb=bnb: v.bn_stats(
                            bnb[:, j * 6:(j + 1) * 6], xt[bx][:, j * 512:(j + 1) * 512]),
                            wait=([("ws", x_rdy), ("dv", ci_free[ci][0]),
                                   ("ac", ci_free[ci][1])] if j == 0 else ()),
                            inc=[("dv", 1)] if j == 3 else ())
                    dv_bn = pr.val("dv")
                    dv_var = pr.emit("vector", lambda v, bnb=bnb, ci=ci: v.bn_aggr(
                        lncol[:, 4 * ci:4 * ci + 2], bnb),
                        wait=[("dv", dv_bn)], inc=[("dv", 1)])["dv"]
                    pr.emit("scalar", lambda s, a=c_var: s.activation(a, a, AF.Ln, bias=B_EPS),
                            wait=[("dv", dv_var)])
                    ac_rstd = pr.emit("scalar", lambda s, a=c_var, b=c_rstd: s.activation(
                        b, a, AF.Exp, scale=-0.5), inc=[("ac", 1)])["ac"]
                    ln_state[t] = (bx, ci, ac_rstd)

                def emit_norm(t):
                    bx, ci, ac_rstd = ln_state.pop(t)
                    bi = t % 2
                    CS = lncol[:, 4 * ci:4 * ci + 4]
                    c_mean, c_rstd = CS[:, 0:1], CS[:, 3:4]
                    dv_norm = pr.emit("vector", lambda v, bi=bi, bx=bx, a=c_mean, b=c_rstd:
                                      v.tensor_scalar(xn_tm[bi][:], xt[bx][:], a, b,
                                                      op0=ALU.subtract, op1=ALU.mult),
                                      wait=[("ac", ac_rstd), ("pe", xtm_free[bi])],
                                      inc=[("dv", 1)])["dv"]
                    xt_free[bx] = (dv_norm, 0)
                    ci_free[ci] = (dv_norm, ac_rstd)
                    ln_state[("n", t)] = dv_norm

                def emit_rest(t, own):
                    nonlocal xnf32_free, glog_free, gtp_free
                    bi = t % 2
                    ti = (t // 4) % 2
                    k4 = t % 4
                    dv_norm = ln_state.pop(("n", t))
                    last_tp = 0
                    for dc in range(16):
                        pk = dc % 2
                        pe_tp = pr.emit("tensor", lambda tt, bi=bi, dc=dc, pk=pk: tt.transpose(
                            pb[pk][:, 0:128], xn_tm[bi][:, _ts(dc)], ident_f[:]),
                            wait=[("dv", dv_norm), ("dv", tp_free[pk][0]),
                                  ("ac", tp_free[pk][1])],
                            inc=[("pe", 1)])["pe"]
                        last_tp = pe_tp
                        dv_ev = pr.emit("vector", lambda v, ti=ti, dc=dc, k4=k4, pk=pk:
                                        v.tensor_copy(
                            xn_grp[ti][:, dc * 512 + k4 * 128: dc * 512 + k4 * 128 + 128],
                            pb[pk][:, 0:128]),
                            wait=[("pe", pe_tp), ("pe", xng_free[ti]),
                                  ("ws", xng_ws_free[ti])],
                            inc=[("dv", 1)])["dv"]
                        ac_ev = tp_free[pk][1]
                        if own:
                            ac_ev = pr.emit("scalar", lambda s, dc=dc, pk=pk: s.activation(
                                xnf32[:, _ts(dc)], pb[pk][:, 0:128], AF.Identity),
                                wait=[("pe", pe_tp), ("pe", xnf32_free), ("dv", dv_ev)],
                                inc=[("ac", 1)])["ac"]
                        tp_free[pk] = (dv_ev, ac_ev)
                    xtm_free[bi] = last_tp
                    if own:
                        last_ac = max(tp_free[0][1], tp_free[1][1])
                        for dc in range(16):
                            pr.emit("tensor", lambda tt, dc=dc: tt.matmul(
                                pb[2][0:16, 0:128], gw_sb[:, 16 * dc:16 * dc + 16],
                                xnf32[:, _ts(dc)], start=(dc == 0), stop=(dc == 15),
                                skip_group_check=True),
                                wait=([("ac", last_ac), ("dv", glog_free)]
                                      if dc == 0 else ()),
                                inc=[("pe", 1)] if dc == 15 else ())
                        pe_g = pr.val("pe")
                        xnf32_free = pe_g
                        dv_g = pr.emit("vector", lambda v: v.tensor_copy(
                            lg_sb[:], pb[2][0:16, 0:128]),
                            wait=[("pe", pe_g)], inc=[("dv", 1)])["dv"]
                        glog_free = dv_g
                        pe_gt = pr.emit("tensor", lambda tt: tt.transpose(
                            pb[3][0:128, 0:16], lg_sb[:], ident_f[0:16, 0:16]),
                            wait=[("dv", dv_g), ("dv", gtp_free)], inc=[("pe", 1)])["pe"]
                        dv_lt = pr.emit("vector", lambda v: v.tensor_copy(
                            lt_sb[:], pb[3][0:128, 0:16]),
                            wait=[("pe", pe_gt)], inc=[("dv", 1)])["dv"]
                        g_m1, g_m2 = gcol[:, 0:1], gcol[:, 1:2]
                        g_nm1, g_rs = gcol[:, 2:3], gcol[:, 3:4]
                        dv_pre = _chain(pr, "vector", [
                            lambda v, a=g_m1: v.reduce_max(a, lt_sb[:], axis=AX.X),
                            lambda v, a=g_m1: v.tensor_scalar(
                                gm_sb[:], lt_sb[:], a, None, op0=ALU.is_equal),
                            lambda v: v.scalar_tensor_tensor(
                                gm_sb[:], gm_sb[:], -1e30, lt_sb[:],
                                op0=ALU.mult, op1=ALU.add),
                            lambda v, a=g_m2: v.reduce_max(a, gm_sb[:], axis=AX.X),
                            lambda v, a=g_m2: v.tensor_scalar(
                                gm_sb[:], lt_sb[:], a, None, op0=ALU.is_ge),
                            lambda v, a=g_m1, b=g_nm1: v.tensor_scalar_mul(b, a, -1.0),
                        ], wait0=[("dv", dv_lt)])
                        ac_ex = pr.emit("scalar", lambda s, b=g_nm1: s.activation(
                            lt_sb[:], lt_sb[:], AF.Exp, bias=b),
                            wait=[("dv", dv_pre)], inc=[("ac", 1)])["ac"]
                        dv_cmb = _chain(pr, "vector", [
                            lambda v, a=g_rs: v.reduce_sum(a, lt_sb[:], axis=AX.X),
                            lambda v, a=g_rs: v.reciprocal(a, a),
                            lambda v: v.tensor_mul(gm_sb[:], gm_sb[:], lt_sb[:]),
                            lambda v, a=g_rs: v.tensor_scalar_mul(gm_sb[:], gm_sb[:], a),
                        ], wait0=[("ac", ac_ex)])
                        pe_ct = pr.emit("tensor", lambda tt: tt.transpose(
                            pb[3][0:16, 0:128], gm_sb[:], ident_f[:]),
                            wait=[("dv", dv_cmb)], inc=[("pe", 1)])["pe"]
                        dv_ce = pr.emit("vector", lambda v: v.tensor_copy(
                            lg_sb[0:16, 0:128], pb[3][0:16, 0:128]),
                            wait=[("pe", pe_ct), ("dv", dv_lt)], inc=[("dv", 1)])["dv"]
                        gtp_free = dv_ce
                        pr.emit("sync", lambda e, t=t: e.dma_start(
                            out=comb_hbm[:, t * 128:(t + 1) * 128], in_=lg_sb[0:16, 0:128]),
                            wait=[("dv", dv_ce)], inc=[("ws", 16)])

                for grp in range(8):
                    own = grp < 4
                    gl = grp % 4
                    src = x_own if own else x_oth
                    base_col = PM_GO[gl] if own else PM_GT[gl]
                    ti = grp % 2
                    ck, sk = tabs[ti]
                    tab_ready[ti] = rope_tables(pr, base_col, 512, ck, sk, genS, for_q=False)
                    if grp == 0 and "tab" in dbg_outs:
                        pr.emit("sync", lambda e: e.dma_start(
                            out=dbg_outs["tab"][:, 0:512], in_=ck[:]),
                            wait=[("dv", tab_ready[0])], inc=[("ws", 16)])
                        pr.emit("sync", lambda e: e.dma_start(
                            out=dbg_outs["tab"][:, 512:1024], in_=sk[:]), inc=[("ws", 16)])
                        pr.emit("sync", lambda e: e.dma_start(
                            out=dbg_outs["tab"][:, 1024:1536], in_=genS[0][:]), inc=[("ws", 16)])
                        pr.emit("sync", lambda e: e.dma_start(
                            out=dbg_outs["tab"][:, 1536:1548], in_=cols_c[:]), inc=[("ws", 16)])
                        pr.emit("sync", lambda e: e.dma_start(
                            out=dbg_outs["tab"][:, 1548:1549], in_=iota_p[:],
                            allow_slow_non_contiguous=True), inc=[("ws", 16)])
                        pr.emit("vector", lambda v: v.nop(), wait=[("ws", pr.val("ws"))])
                        pr.emit("gpsimd", lambda g: g.nop(), wait=[("ws", pr.val("ws"))])
                    grp_xn_dv = 0
                    for k4 in range(4):
                        t = grp * 4 + k4
                        if t == 0:
                            emit_ln(0, x_own, 0)
                            emit_ln(1, x_own, 1)
                            emit_norm(0)
                        if t + 1 < 32:
                            emit_norm(t + 1)
                        emit_rest(t, grp < 4)
                        grp_xn_dv = pr.val("dv")
                        if t + 2 < 32:
                            nsrc = x_own if t + 2 < 16 else x_oth
                            emit_ln(t + 2, nsrc, (t + 2) % 16)
                    for kv in range(NKV):
                        for dc in range(16):
                            pr.emit("tensor", lambda tt, kv=kv, dc=dc, ti=ti: tt.matmul(
                                pb[4][:], wk_bf[:, (kv * 16 + dc) * 128:(kv * 16 + dc + 1) * 128],
                                xn_grp[ti][:, dc * 512:(dc + 1) * 512],
                                start=(dc == 0), stop=(dc == 15), skip_group_check=True),
                                wait=([("dv", grp_xn_dv), ("wg", wg_kv),
                                       ("ac", pb4_free_ac)] if dc == 0 else ()),
                                inc=[("pe", 1)] if dc == 15 else ())
                        pe_k = pr.val("pe")
                        ac_k = pr.emit("scalar", lambda s: s.activation(
                            kv_sb[0][:], pb[4][:], AF.Identity),
                            wait=[("pe", pe_k), ("pe", kvsb0_free_pe),
                                  ("dv", kvsb0_free_dv)], inc=[("ac", 1)])["ac"]
                        pe_sw = pr.emit("tensor", lambda tt: tt.matmul(
                            pb[5][:], swap_bf[:], kv_sb[0][:], start=True, stop=True,
                            skip_group_check=True),
                            wait=[("ac", ac_k)], inc=[("pe", 1)])["pe"]
                        kvsb0_free_pe = pe_sw
                        ac_sw = pr.emit("scalar", lambda s: s.activation(
                            kv_sw[:], pb[5][:], AF.Identity),
                            wait=[("pe", pe_sw), ("dv", kvsw_free_dv)], inc=[("ac", 1)])["ac"]
                        pr.emit("vector", lambda v, ck=ck: v.tensor_mul(kv_t1[:], kv_sb[0][:], ck[:]),
                                wait=[("ac", ac_sw), ("dv", tab_ready[ti])])
                        pr.emit("vector", lambda v, sk=sk: v.tensor_mul(kv_t2[:], kv_sw[:], sk[:]))
                        col0 = kv * S + (0 if own else TOWN) + gl * 512
                        dv_kr = pr.emit("vector", lambda v, col0=col0: v.tensor_add(
                            krope[:, col0:col0 + 512], kv_t1[:], kv_t2[:]),
                            inc=[("dv", 1)])["dv"]
                        kvsb0_free_dv = dv_kr
                        kvsw_free_dv = dv_kr
                        for dc in range(16):
                            pr.emit("tensor", lambda tt, kv=kv, dc=dc, ti=ti: tt.matmul(
                                pb[4][:], wv_bf[:, (kv * 16 + dc) * 128:(kv * 16 + dc + 1) * 128],
                                xn_grp[ti][:, dc * 512:(dc + 1) * 512],
                                start=(dc == 0), stop=(dc == 15), skip_group_check=True),
                                wait=[("ac", ac_k)] if dc == 0 else (),
                                inc=[("pe", 1)] if dc == 15 else ())
                        pe_v = pr.val("pe")
                        ac_v = pr.emit("scalar", lambda s: s.activation(
                            kv_sb[1][:], pb[4][:], AF.Identity),
                            wait=[("pe", pe_v), ("pe", kvsb1_free_pe)],
                            inc=[("ac", 1)])["ac"]
                        pb4_free_ac = ac_v
                        for k4 in range(4):
                            tile_idx = (0 if own else 16) + gl * 4 + k4
                            pe_tv = pr.emit("tensor", lambda tt, k4=k4: tt.transpose(
                                pbt[:, 0:128], kv_sb[1][:, _ts(k4)], ident_bf[:]),
                                wait=[("ac", ac_v), ("dv", pbt_free)],
                                inc=[("pe", 1)])["pe"]
                            kvsb1_free_pe = pe_tv
                            pbt_free = pr.emit("vector", lambda v, kv=kv, tile_idx=tile_idx:
                                               v.tensor_copy(
                                vtm[:, kv * S + tile_idx * 128: kv * S + (tile_idx + 1) * 128],
                                pbt[:, 0:128]),
                                wait=[("pe", pe_tv)], inc=[("dv", 1)])["dv"]
                    xng_free[ti] = pr.val("pe")
                    if own:
                        ws_sp = 0
                        for dc in range(16):
                            ws_sp = pr.emit("sync", lambda e, dc=dc, gl=gl, ti=ti: e.dma_start(
                                out=xn_hbm[dc, :, gl * 512:(gl + 1) * 512],
                                in_=xn_grp[ti][:, dc * 512:(dc + 1) * 512]),
                                wait=[("dv", grp_xn_dv)], inc=[("ws", 16)])["ws"]
                        xng_ws_free[ti] = ws_sp
                pr.fence()
                pr.flush(block, sems, waited)

            o_res = esAB.enter_context(nc.sbuf_tensor("o_res", [128, 16 * TOWN], BF16))
            esB = ExitStack()
            with esB:
                def sbb(name, shape, dtype):
                    return esB.enter_context(nc.sbuf_tensor(name, shape, dtype))

                xn_qb = [sbb(f"xn_qb{i}", [128, 16 * 512], BF16) for i in range(2)]
                wq_bf = [sbb(f"wq_bf{i}", [128, 2048], BF16) for i in range(2)]
                cq_t = sbb("cq_t", [128, 512], F32)
                sq_t = sbb("sq_t", [128, 512], F32)
                genB = [sbb(f"genB{i}", [128, 512], F32) for i in range(4)]
                masks = sbb("masks", [128, 8 * 512], BF16)
                posq = sbb("posq", [128, 512], F32)
                kofs = sbb("kofs", [128, 1], F32)
                q_sb = sbb("q_sb", [128, 512], BF16)
                qs_sb = sbb("qs_sb", [128, 512], BF16)
                qr_t1 = sbb("qr_t1", [128, 512], F32)
                qr_sb = sbb("qr_sb", [128, 512], BF16)
                pt_sb = [sbb(f"pt_sb{i}", [128, 512], BF16) for i in range(4)]
                rl_sb = sbb("rl_sb", [1, 512], F32)
                rlb_sb = sbb("rlb_sb", [128, 512], F32)

                pr = Prog(sched, on="B" in phases)
                pr.fence()
                sbanks = [pb[0], pb[1], pb[2], pb[3]]
                BAV, BL, BRB = pb[4], pb[5], pb[6]
                xnq_ready = [0, 0]
                xnq_free = [0, 0]
                wq_ready = [0, 0]
                wq_free = [0, 0]
                pt_free = [0, 0, 0, 0]
                sbank_free = [0, 0, 0, 0]
                av_free = 0
                l_free = 0
                rb_free = 0
                qr_free = 0
                pend_fin = None

                def emit_finalize_a(pe_done):
                    nonlocal l_free
                    dv_rl = pr.emit("vector", lambda v: v.reciprocal(rl_sb[:], BL[0:1, :]),
                                    wait=[("pe", pe_done)], inc=[("dv", 1)])["dv"]
                    l_free = dv_rl
                    return dv_rl

                def emit_finalize_b(h, qb_, dv_rl):
                    nonlocal av_free, rb_free
                    pe_rb = pr.emit("tensor", lambda tt: tt.matmul(
                        BRB[:], ones_row_f[:], rl_sb[:], start=True, stop=True,
                        skip_group_check=True),
                        wait=[("dv", dv_rl), ("dv", rb_free)], inc=[("pe", 1)])["pe"]
                    dv_rb = pr.emit("vector", lambda v: v.tensor_copy(rlb_sb[:], BRB[:]),
                                    wait=[("pe", pe_rb)], inc=[("dv", 1)])["dv"]
                    rb_free = dv_rb
                    dv_o = pr.emit("vector", lambda v, h=h, qb_=qb_: v.tensor_mul(
                        o_res[:, h * TOWN + qb_ * 512: h * TOWN + qb_ * 512 + 512],
                        BAV[:], rlb_sb[:]), inc=[("dv", 1)])["dv"]
                    av_free = dv_o

                def load_xnqb(qb, bi2):
                    ev = 0
                    for dc in range(16):
                        ev = pr.emit("sync", lambda e, dc=dc, qb=qb, bi2=bi2: e.dma_start(
                            out=xn_qb[bi2][:, dc * 512:(dc + 1) * 512],
                            in_=xn_hbm[dc, :, qb * 512:(qb + 1) * 512]),
                            wait=[("pe", xnq_free[bi2])], inc=[("ws", 16)])["ws"]
                    xnq_ready[bi2] = ev

                def load_wq(h, bi2):
                    wq_ready[bi2] = pr.emit("gpsimd", lambda g, h=h, bi2=bi2: g.dma_start(
                        out=wq_bf[bi2][:].rearrange("p (c j) -> p c j", c=16), in_=wq_r[h].rearrange("c p j -> p c j")),
                        wait=[("pe", wq_free[bi2])], inc=[("wg", 16)])["wg"]

                load_xnqb(0, 0)
                load_wq(0, 0)
                for qb in range(4):
                    bi = qb % 2
                    if qb < 3:
                        load_xnqb(qb + 1, (qb + 1) % 2)
                    tabq_dv = rope_tables(pr, PM_GO[qb], 512, cq_t, sq_t, genB, for_q=True)
                    pr.emit("gpsimd", lambda g: g.iota(
                        posq[:], pattern=[[1, 512]], base=0, channel_multiplier=0,
                        allow_small_or_imprecise_dtypes=True),
                        wait=[("dv", pr.val("dv"))], inc=[("go", 1)])
                    gv = pr.val("go")
                    pr.emit("vector", lambda v, qb=qb: v.tensor_scalar_add(
                        posq[:], posq[:], PM_GO[qb]), wait=[("go", gv)])
                    mask_dv = 0
                    for part in range(2):
                        basec = PM_GO[qb] if part == 0 else PM_GT[qb]
                        for kr in range(4):
                            mi = part * 4 + kr
                            mask_dv = _chain(pr, "vector", [
                                lambda v, basec=basec: v.tensor_add(
                                    kofs[:], basec, iota_p[:]),
                                lambda v, kr=kr: v.tensor_scalar_add(
                                    kofs[:], kofs[:], float(128 * kr)),
                                lambda v, mi=mi: v.tensor_scalar(
                                    masks[:, mi * 512:(mi + 1) * 512], posq[:], kofs[:], None,
                                    op0=ALU.is_ge),
                            ])
                    nch = 4 * (qb + 1)
                    for h in range(NH):
                        kv = h // 4
                        wbi = h % 2
                        if h < NH - 1:
                            load_wq(h + 1, (h + 1) % 2)
                        elif qb < 3:
                            load_wq(0, 0)
                        if pend_fin is not None:
                            fin_rl = emit_finalize_a(pend_fin[2])
                        for dc in range(16):
                            pr.emit("tensor", lambda tt, dc=dc, bi=bi, wbi=wbi: tt.matmul(
                                sbanks[2][:], wq_bf[wbi][:, _ts(dc)],
                                xn_qb[bi][:, dc * 512:(dc + 1) * 512],
                                start=(dc == 0), stop=(dc == 15), skip_group_check=True),
                                wait=([("wg", wq_ready[wbi]), ("ws", xnq_ready[bi]),
                                       ("ac", sbank_free[2])] if dc == 0 else ()),
                                inc=[("pe", 1)] if dc == 15 else ())
                        pe_q = pr.val("pe")
                        wq_free[wbi] = pe_q
                        ac_q = pr.emit("scalar", lambda s: s.activation(
                            q_sb[:], sbanks[2][:], AF.Identity),
                            wait=[("pe", pe_q), ("pe", qr_free)], inc=[("ac", 1)])["ac"]
                        sbank_free[2] = ac_q
                        if pend_fin is not None:
                            emit_finalize_b(pend_fin[0], pend_fin[1], fin_rl)
                            pend_fin = None
                        pe_sw = pr.emit("tensor", lambda tt: tt.matmul(
                            sbanks[3][:], swap_bf[:], q_sb[:], start=True, stop=True,
                            skip_group_check=True),
                            wait=[("ac", ac_q), ("ac", sbank_free[3])], inc=[("pe", 1)])["pe"]
                        ac_sw = pr.emit("scalar", lambda s: s.activation(
                            qs_sb[:], sbanks[3][:], AF.Identity),
                            wait=[("pe", pe_sw)], inc=[("ac", 1)])["ac"]
                        sbank_free[3] = ac_sw
                        pr.emit("vector", lambda v: v.tensor_mul(qr_t1[:], q_sb[:], cq_t[:]),
                                wait=[("ac", ac_sw), ("dv", tabq_dv), ("pe", qr_free)])
                        pr.emit("vector", lambda v: v.tensor_mul(qr_sb[:], qs_sb[:], sq_t[:]))
                        dv_qr = pr.emit("vector", lambda v: v.scalar_tensor_tensor(
                            qr_sb[:], qr_sb[:], 1.0, qr_t1[:], op0=ALU.mult, op1=ALU.add),
                            inc=[("dv", 1)])["dv"]
                        chunks = [(p_, kc) for p_ in range(2) for kc in range(nch)]
                        nchk = len(chunks)
                        exp_ev = [None] * nchk
                        pchunks = []

                        def emit_st(idx):
                            part, kc = chunks[idx]
                            col0 = kv * S + (0 if part == 0 else TOWN) + kc * 128
                            bk = idx % 4
                            pe_s = pr.emit("tensor", lambda tt, col0=col0, bk=bk: tt.matmul(
                                sbanks[bk][:], krope[:, col0:col0 + 128], qr_sb[:],
                                start=True, stop=True, skip_group_check=True),
                                wait=[("dv", dv_qr), ("ac", sbank_free[bk])],
                                inc=[("pe", 1)])["pe"]
                            pi = idx % 4
                            ac_e = pr.emit("scalar", lambda s, bk=bk, pi=pi: s.activation(
                                pt_sb[pi][:], sbanks[bk][:], AF.Exp, bias=B_NSH),
                                wait=[("pe", pe_s), ("pe", pt_free[pi])],
                                inc=[("ac", 1)])["ac"]
                            sbank_free[bk] = ac_e
                            if kc >= nch - 4:
                                mi = part * 4 + (kc - (nch - 4))
                                dv_m = pr.emit("vector", lambda v, pi=pi, mi=mi: v.tensor_mul(
                                    pt_sb[pi][:], pt_sb[pi][:],
                                    masks[:, mi * 512:(mi + 1) * 512]),
                                    wait=[("ac", ac_e), ("dv", mask_dv)],
                                    inc=[("dv", 1)])["dv"]
                                exp_ev[idx] = ("dv", dv_m)
                            else:
                                exp_ev[idx] = ("ac", ac_e)

                        def emit_pvl(idx):
                            part, kc = chunks[idx]
                            tile_idx = (0 if part == 0 else 16) + kc
                            pi = idx % 4
                            first = idx == 0
                            last = idx == nchk - 1
                            pr.emit("tensor", lambda tt, kv=kv, tile_idx=tile_idx, pi=pi,
                                    first=first, last=last: tt.matmul(
                                BAV[:],
                                vtm[:, kv * S + tile_idx * 128: kv * S + (tile_idx + 1) * 128],
                                pt_sb[pi][:], start=first, stop=last, skip_group_check=True),
                                wait=[exp_ev[idx]] + ([("dv", av_free)] if first else []),
                                inc=[("pe", 1)])
                            pe_l = pr.emit("tensor", lambda tt, pi=pi, first=first, last=last:
                                           tt.matmul(
                                BL[0:1, :], ones_col_bf[:], pt_sb[pi][:],
                                start=first, stop=last, skip_group_check=True),
                                wait=[("dv", l_free)] if first else (),
                                inc=[("pe", 1)])["pe"]
                            pt_free[pi] = pe_l

                        for idx in range(nchk):
                            emit_st(idx)
                            if idx >= 3:
                                emit_pvl(idx - 3)
                        emit_pvl(nchk - 3)
                        emit_pvl(nchk - 2)
                        emit_pvl(nchk - 1)
                        qr_free = pr.val("pe")
                        pend_fin = (h, qb, pr.val("pe"))
                    emit_finalize_b(pend_fin[0], pend_fin[1], emit_finalize_a(pend_fin[2]))
                    pend_fin = None
                    xnq_free[bi] = pr.val("pe")
                pr.fence()
                pr.flush(block, sems, waited)

                if "krope" in dbg_outs:
                    prd = Prog(sched)
                    prd.emit("sync", lambda e: e.dma_start(out=dbg_outs["krope"], in_=krope[:]),
                             inc=[("ws", 16)])
                    prd.emit("sync", lambda e: e.dma_start(out=dbg_outs["vtm"], in_=vtm[:]),
                             inc=[("ws", 16)])
                    prd.fence(("sync",))
                    prd.flush(block, sems, waited)

            esB2 = ExitStack()
            with esB2:
                wo_bf = [esB2.enter_context(nc.sbuf_tensor(f"wo_bf{i}", [128, 2048], BF16))
                         for i in range(2)]
                s_sb = [esB2.enter_context(nc.sbuf_tensor(f"s_sb{i}", [128, 512], BF16))
                        for i in range(2)]
                pr = Prog(sched, on="B2" in phases)
                pr.fence()
                wo_ready = [0, 0]
                wo_free = [0, 0]
                s_free = [0, 0]
                s_ws_free = [0, 0]

                def load_wo(od, wbi2):
                    wo_ready[wbi2] = pr.emit("gpsimd", lambda g, od=od, wbi2=wbi2: g.dma_start(
                        out=wo_bf[wbi2][:].rearrange("p (c j) -> p c j", c=16), in_=wo_r[od].rearrange("c p j -> p c j")),
                        wait=[("pe", wo_free[wbi2])], inc=[("wg", 16)])["wg"]

                load_wo(0, 0)
                load_wo(1, 1)
                for od in range(16):
                    wbi = od % 2
                    for tg in range(4):
                        bk = pb[tg % 2]
                        si = tg % 2
                        for dc in range(16):
                            pr.emit("tensor", lambda tt, wbi=wbi, dc=dc, tg=tg, bk=bk: tt.matmul(
                                bk[:], wo_bf[wbi][:, _ts(dc)],
                                o_res[:, dc * TOWN + tg * 512: dc * TOWN + tg * 512 + 512],
                                start=(dc == 0), stop=(dc == 15), skip_group_check=True),
                                wait=([("wg", wo_ready[wbi]), ("dv", s_free[si])]
                                      if dc == 0 else ()),
                                inc=[("pe", 1)] if dc == 15 else ())
                        pe_s = pr.val("pe")
                        if tg == 3:
                            wo_free[wbi] = pe_s
                            if od + 2 < 16:
                                load_wo(od + 2, wbi)
                        dv_s = pr.emit("vector", lambda v, si=si, bk=bk: v.tensor_copy(
                            s_sb[si][:], bk[:]),
                            wait=[("pe", pe_s), ("ws", s_ws_free[si])], inc=[("dv", 1)])["dv"]
                        ws_s = pr.emit("sync", lambda e, si=si, od=od, tg=tg: e.dma_start(
                            out=s_hbm[od, :, tg * 512:(tg + 1) * 512], in_=s_sb[si][:]),
                            wait=[("dv", dv_s)], inc=[("ws", 16)])["ws"]
                        s_free[si] = dv_s
                        s_ws_free[si] = ws_s
                pr.fence()
                pr.flush(block, sems, waited)
                if "o" in dbg_outs:
                    prd = Prog(sched)
                    prd.emit("sync", lambda e: e.dma_start(out=dbg_outs["o"], in_=o_res[:]),
                             inc=[("ws", 16)])
                    prd.fence(("sync",))
                    prd.flush(block, sems, waited)

        esC = ExitStack()
        with esC:
            def sbc(name, shape, dtype):
                return esC.enter_context(nc.sbuf_tensor(name, shape, dtype))

            xn_hf = sbc("xn_hf", [128, 16 * HALF], BF16)
            y_hf = sbc("y_hf", [128, 16 * HALF], F32)
            y_ready_dv = 0
            for hf in range(2):
                esCi = ExitStack()
                with esCi:
                    def sbi(name, shape, dtype):
                        return esCi.enter_context(nc.sbuf_tensor(name + str(hf), shape, dtype))

                    h_buf = sbi("h_buf", [128, 11 * HALF], BF16)
                    wgu_bf = [sbi(f"wgu_bf{i}", [128, 2048], BF16) for i in range(4)]
                    wd_full = sbi("wd_full", [128, 11 * 2048], BF16)
                    cb_sb = [sbi(f"cb_sb{i}", [128, HALF], F32) for i in range(2)]
                    crow = sbi("crow", [1, HALF], F32)
                    sg_sb = [sbi(f"sg_sb{i}", [128, 512], BF16) for i in range(2)]
                    uu_sb = [sbi(f"uu_sb{i}", [128, 512], BF16) for i in range(2)]

                    pr = Prog(sched, on="C" in phases)
                    pr.fence()
                    xn_ready = 0
                    for dc in range(16):
                        xn_ready = pr.emit("sync", lambda e, dc=dc, hf=hf: e.dma_start(
                            out=xn_hf[:, dc * HALF:(dc + 1) * HALF],
                            in_=xn_hbm[dc, :, hf * HALF:(hf + 1) * HALF]),
                            inc=[("ws", 16)])["ws"]

                    wgu_ready = [0, 0, 0, 0]
                    wgu_free = [0, 0, 0, 0]
                    wd_ready = 0
                    wd_free = 0
                    cb_ready = [0, 0]
                    crow_free = 0
                    pb6_free = 0
                    g_bank_free = [0, 0]
                    u_bank_free = [0, 0]
                    d_bank_free = [0, 0]
                    sg_free = [0, 0]
                    h_read_pe = 0
                    gu_slot = [0]

                    def load_wgu(e_i, jc):
                        wbi = gu_slot[0] % 4
                        gu_slot[0] += 1
                        src = wgu_r[e_i, jc] if e_i < EXPERTS else sgu_r[jc]
                        wgu_ready[wbi] = pr.emit("gpsimd", lambda g, src=src, wbi=wbi: g.dma_start(
                            out=wgu_bf[wbi][:].rearrange("p (c j) -> p c j", c=16), in_=src.rearrange("c p j -> p c j")),
                            wait=[("pe", wgu_free[wbi])], inc=[("wg", 16)])["wg"]
                        return wbi

                    def load_wd(e_i, hpass):
                        nonlocal wd_ready
                        for ic in range(11):
                            src = (wd_r[e_i, ic] if e_i < EXPERTS
                                   else sdn_r[hpass * 11 + ic])
                            wd_ready = pr.emit("gpsimd", lambda g, src=src, ic=ic: g.dma_start(
                                out=wd_full[:, ic * 2048:(ic + 1) * 2048], in_=src),
                                wait=[("pe", wd_free)] if ic == 0 else (),
                                inc=[("wg", 16)])["wg"]

                    def load_cb(e_i, cbi):
                        nonlocal crow_free, pb6_free
                        ws_c = pr.emit("sync", lambda e, e_i=e_i, hf=hf: e.dma_start(
                            out=crow[:], in_=comb_hbm[e_i:e_i + 1, hf * HALF:(hf + 1) * HALF]),
                            wait=[("pe", crow_free)], inc=[("ws", 16)])["ws"]
                        for h2 in range(2):
                            pe_b = pr.emit("tensor", lambda tt, h2=h2: tt.matmul(
                                pb[6][:], ones_row_f[:], crow[0:1, h2 * 512:(h2 + 1) * 512],
                                start=True, stop=True, skip_group_check=True),
                                wait=[("ws", ws_c), ("dv", pb6_free)], inc=[("pe", 1)])["pe"]
                            crow_free = pe_b
                            pb6_free = pr.emit("vector", lambda v, cbi=cbi, h2=h2: v.tensor_copy(
                                cb_sb[cbi][:, h2 * 512:(h2 + 1) * 512], pb[6][:]),
                                wait=[("pe", pe_b)], inc=[("dv", 1)])["dv"]
                        cb_ready[cbi] = pb6_free

                    units = [(e, 0) for e in range(EXPERTS)] + [(EXPERTS, 0), (EXPERTS, 1)]
                    if C_UNITS is not None:
                        units = list(C_UNITS)
                    e0, hp0 = units[0]
                    if e0 < EXPERTS:
                        load_cb(e0, e0 % 2)
                    load_wgu(e0, hp0 * 11 if e0 == EXPERTS else 0)
                    for ui, (e_i, hpass) in enumerate(units):
                        cbi = e_i % 2
                        h_dv = 0
                        for c in range(11):
                            for pw in range(2):
                                if e_i == EXPERTS:
                                    jc = (hpass * 11 + c) if pw == 0 else (22 + hpass * 11 + c)
                                else:
                                    jc = c if pw == 0 else (11 + c)
                                wbi = (gu_slot[0] - 1) % 4 if (ui == 0 and c == 0 and pw == 0) \
                                    else load_wgu(e_i, jc)
                                if ui == 0 and c == 0 and pw == 0:
                                    wbi = 0
                                myrdy = wgu_ready[wbi]
                                for tg in range(2):
                                    bk = (pb[0], pb[1])[tg] if pw == 0 else (pb[2], pb[3])[tg]
                                    bfree = ([("ac", g_bank_free[tg])] if pw == 0
                                             else [("dv", u_bank_free[tg])])
                                    for dc in range(16):
                                        pr.emit("tensor", lambda tt, dc=dc, tg=tg, wbi=wbi, bk=bk:
                                                tt.matmul(
                                            bk[:], wgu_bf[wbi][:, _ts(dc)],
                                            xn_hf[:, dc * HALF + tg * 512: dc * HALF + tg * 512 + 512],
                                            start=(dc == 0), stop=(dc == 15),
                                            skip_group_check=True),
                                            wait=([("wg", myrdy), ("ws", xn_ready)] + bfree
                                                  if dc == 0 else ()),
                                            inc=[("pe", 1)] if dc == 15 else ())
                                    pe_gu = pr.val("pe")
                                    if tg == 1:
                                        wgu_free[wbi] = pe_gu
                                    if pw == 0:
                                        g_bank_free[tg] = pr.emit(
                                            "scalar", lambda s, tg=tg, bk=bk: s.activation(
                                                sg_sb[tg][:], bk[:], AF.Silu),
                                            wait=[("pe", pe_gu), ("dv", sg_free[tg])],
                                            inc=[("ac", 1)])["ac"]
                                    else:
                                        dv_u = pr.emit("vector", lambda v, tg=tg, bk=bk:
                                                       v.tensor_copy(uu_sb[tg][:], bk[:]),
                                                       wait=[("pe", pe_gu)],
                                                       inc=[("dv", 1)])["dv"]
                                        u_bank_free[tg] = dv_u
                                        col = c * HALF + tg * 512
                                        h_dv = pr.emit("vector", lambda v, tg=tg, col=col:
                                                       v.tensor_mul(
                                            h_buf[:, col:col + 512], sg_sb[tg][:], uu_sb[tg][:]),
                                            wait=[("ac", g_bank_free[tg]),
                                                  ("dv", cb_ready[cbi]),
                                                  ("pe", h_read_pe)],
                                            inc=[("dv", 1)])["dv"]
                                        if e_i < EXPERTS:
                                            h_dv = pr.emit("vector", lambda v, tg=tg, col=col,
                                                           cbi=cbi: v.tensor_mul(
                                                h_buf[:, col:col + 512],
                                                h_buf[:, col:col + 512],
                                                cb_sb[cbi][:, tg * 512:(tg + 1) * 512]),
                                                inc=[("dv", 1)])["dv"]
                                        sg_free[tg] = h_dv
                        if e_i + 1 <= EXPERTS and hpass == 0 and e_i + 1 < EXPERTS:
                            load_cb(e_i + 1, (e_i + 1) % 2)
                        load_wd(e_i, hpass)
                        for od in range(16):
                            for tg in range(2):
                                dbk = (od * 2 + tg) % 2
                                bk = pb[4 + dbk]
                                for ic in range(11):
                                    pr.emit("tensor", lambda tt, ic=ic, od=od, tg=tg, bk=bk:
                                            tt.matmul(
                                        bk[:], wd_full[:, ic * 2048 + od * 128: ic * 2048 + (od + 1) * 128],
                                        h_buf[:, ic * HALF + tg * 512: ic * HALF + tg * 512 + 512],
                                        start=(ic == 0), stop=(ic == 10),
                                        skip_group_check=True),
                                        wait=([("wg", wd_ready), ("dv", h_dv),
                                               ("dv", d_bank_free[dbk])] if ic == 0 else ()),
                                        inc=[("pe", 1)] if ic == 10 else ())
                                pe_d = pr.val("pe")
                                ysl = y_hf[:, od * HALF + tg * 512: od * HALF + tg * 512 + 512]
                                if ui == 0:
                                    dv_y = pr.emit("vector", lambda v, ysl=ysl, bk=bk:
                                                   v.tensor_copy(ysl, bk[:]),
                                                   wait=[("pe", pe_d)], inc=[("dv", 1)])["dv"]
                                else:
                                    dv_y = pr.emit("vector", lambda v, ysl=ysl, bk=bk:
                                                   v.tensor_add(ysl, bk[:], ysl),
                                                   wait=[("pe", pe_d)], inc=[("dv", 1)])["dv"]
                                d_bank_free[dbk] = dv_y
                                y_ready_dv = dv_y
                        h_read_pe = pr.val("pe")
                        wd_free = h_read_pe
                    pr.fence()
                    pr.flush(block, sems, waited)
                    if ("y%d" % hf) in dbg_outs:
                        prd = Prog(sched)
                        prd.emit("sync", lambda e, hf=hf: e.dma_start(
                            out=dbg_outs["y%d" % hf], in_=y_hf[:]), inc=[("ws", 16)])
                        prd.fence(("sync",))
                        prd.flush(block, sems, waited)

                esD = ExitStack()
                with esD:
                    def sbd(name, shape, dtype):
                        return esD.enter_context(nc.sbuf_tensor(name + str(hf), shape, dtype))

                    ytm = [sbd(f"ytm{i}", [128, D], F32) for i in range(3)]
                    stm = [sbd(f"stm{i}", [128, D], BF16) for i in range(3)]
                    gtm = [sbd(f"gtm{i}", [128, D], F32) for i in range(3)]
                    xtd = [sbd(f"xtd{i}", [128, D], F32) for i in range(3)]

                    prd = Prog(sched, on="D" in phases)
                    prd.fence()
                    stm_ready = [0, 0, 0]
                    ytm_free = [0, 0, 0]
                    stm_free = [0, 0, 0]
                    gtm_free = [0, 0, 0]
                    xtd_free = [0, 0, 0]
                    ytp_free = [(0, 0), (0, 0), (0, 0), (0, 0)]
                    for tt_i in range(8):
                        bi = tt_i % 3
                        row0 = hf * HALF + tt_i * 128
                        x_rdy = prd.emit("sync", lambda e, bi=bi, row0=row0: e.dma_start(
                            out=xtd[bi][:], in_=x_own[row0:row0 + 128, :]),
                            wait=[("ws", xtd_free[bi])], inc=[("ws", 16)])["ws"]
                        for dc in range(16):
                            stm_ready[bi] = prd.emit("scalar", lambda e, bi=bi, dc=dc, tt_i=tt_i,
                                                     hf=hf: e.dma_start(
                                out=stm[bi][:, _ts(dc)],
                                in_=s_hbm[dc, :, hf * HALF + tt_i * 128:
                                          hf * HALF + tt_i * 128 + 128],
                                transpose=True),
                                wait=[("dv", stm_free[bi])] if dc == 0 else (),
                                inc=[("ws", 16)])["ws"]
                        for dc in range(16):
                            pk = dc % 4
                            pe_y = prd.emit("tensor", lambda tt, dc=dc, tt_i=tt_i, pk=pk:
                                            tt.transpose(
                                pb[pk][:, 0:128],
                                y_hf[:, dc * HALF + tt_i * 128: dc * HALF + tt_i * 128 + 128],
                                ident_f[:]),
                                wait=[("dv", y_ready_dv), ("dv", ytp_free[pk][0])],
                                inc=[("pe", 1)])["pe"]
                            dv_ye = prd.emit("vector", lambda v, bi=bi, dc=dc, pk=pk:
                                             v.tensor_copy(ytm[bi][:, _ts(dc)], pb[pk][:, 0:128]),
                                             wait=[("pe", pe_y), ("ac", ytm_free[bi])],
                                             inc=[("dv", 1)])["dv"]
                            ytp_free[pk] = (dv_ye, 0)
                        dv_tp = prd.val("dv")
                        ac_g = prd.emit("scalar", lambda s, bi=bi: s.activation(
                            gtm[bi][:], ytm[bi][:], AF.Silu),
                            wait=[("dv", dv_tp), ("dv", gtm_free[bi])], inc=[("ac", 1)])["ac"]
                        ytm_free[bi] = ac_g
                        prd.emit("vector", lambda v, bi=bi: v.tensor_mul(
                            gtm[bi][:], stm[bi][:], gtm[bi][:]),
                            wait=[("ac", ac_g), ("ws", x_rdy), ("ws", stm_ready[bi])])
                        dv_out = prd.emit("vector", lambda v, bi=bi: v.tensor_add(
                            xtd[bi][:], gtm[bi][:], xtd[bi][:]), inc=[("dv", 1)])["dv"]
                        stm_free[bi] = dv_out
                        gtm_free[bi] = dv_out
                        ws_out = prd.emit("sync", lambda e, bi=bi, row0=row0: e.dma_start(
                            out=out_t[row0:row0 + 128, :], in_=xtd[bi][:]),
                            wait=[("dv", dv_out)], inc=[("ws", 16)])["ws"]
                        xtd_free[bi] = ws_out
                    prd.fence()
                    prd.flush(block, sems, waited)

        def _final(e):
            e.wait_ge(sems["ws"], sched.get("ws", 0))
            e.wait_ge(sems["wg"], sched.get("wg", 0))
        block.sync(_final)

    return nc


def _prep_shared(Wq, Wk, Wv, Wo, gate_w, w_gate_up, w_down, sgu, sdn):
    d = {}
    d["wq_r"] = np.ascontiguousarray(Wq.reshape(16, 128, NH, 128).transpose(2, 0, 1, 3))
    d["wk_r"] = np.ascontiguousarray(Wk.reshape(16, 128, NKV, 128).transpose(2, 0, 1, 3))
    d["wv_r"] = np.ascontiguousarray(Wv.reshape(16, 128, NKV, 128).transpose(2, 0, 1, 3))
    d["wo_r"] = np.ascontiguousarray(Wo.reshape(16, 128, 16, 128).transpose(2, 0, 1, 3))
    d["gw_r"] = np.ascontiguousarray(gate_w.T.reshape(16, 128, 16))
    d["wgu_r"] = np.ascontiguousarray(
        w_gate_up.reshape(EXPERTS, 16, 128, 22, 128).transpose(0, 3, 1, 2, 4))
    d["wd_r"] = np.ascontiguousarray(w_down.reshape(EXPERTS, 11, 128, D))
    d["sgu_r"] = np.ascontiguousarray(sgu.reshape(16, 128, 44, 128).transpose(2, 0, 1, 3))
    d["sdn_r"] = np.ascontiguousarray(sdn.reshape(22, 128, D))
    return d


def _core_inputs(x, b, m, shared):
    own_rows = np.concatenate([np.arange(512 * (2 * j + m), 512 * (2 * j + m) + 512)
                               for j in range(4)])
    oth_rows = np.concatenate([np.arange(512 * (2 * j + 1 - m), 512 * (2 * j + 1 - m) + 512)
                               for j in range(4)])
    d = dict(shared)
    d["x_own"] = np.ascontiguousarray(x[b][own_rows])
    d["x_oth"] = np.ascontiguousarray(x[b][oth_rows])
    pm = np.zeros((128, 16), dtype=np.float32)
    for j in range(4):
        pm[:, j] = 512 * (2 * j + m)
        pm[:, 4 + j] = 512 * (2 * j + 1 - m)
    d["posmeta"] = pm
    return d, own_rows


def kernel(L0_attn_x, L0_attn_Wq, L0_attn_Wk, L0_attn_Wv, L0_attn_Wo,
           L0_ffn_gate_w, L0_ffn_w_gate_up, L0_ffn_w_down,
           L0_ffn_shared_gate_up, L0_ffn_shared_down):
    x = np.asarray(L0_attn_x, dtype=np.float32)
    shared = _prep_shared(np.asarray(L0_attn_Wq), np.asarray(L0_attn_Wk),
                          np.asarray(L0_attn_Wv), np.asarray(L0_attn_Wo),
                          np.asarray(L0_ffn_gate_w), np.asarray(L0_ffn_w_gate_up),
                          np.asarray(L0_ffn_w_down), np.asarray(L0_ffn_shared_gate_up),
                          np.asarray(L0_ffn_shared_down))
    in_maps = []
    row_maps = []
    for c in range(8):
        d, own_rows = _core_inputs(x, c // 2, c % 2, shared)
        in_maps.append(d)
        row_maps.append(own_rows)
    nc = build_program()
    res = run_bass_kernel_spmd(nc, in_maps, core_ids=list(range(8)))
    out = np.empty((B, S, D), dtype=np.float32)
    for c in range(8):
        out[c // 2][row_maps[c]] = res.results[c]["out"]
    return out
